# version 27
# baseline (speedup 1.0000x reference)
import numpy as np
import ml_dtypes
import concourse.bass as bass
import concourse.mybir as mybir
from concourse.bass_utils import run_bass_kernel_spmd

F32 = mybir.dt.float32
BF16 = mybir.dt.bfloat16
AF = mybir.ActivationFunctionType
ALU = mybir.AluOpType
AX = mybir.AxisListType.X

B, S, D = 2, 8192, 1024
NCORES = 8
E_EXP, HID = 16, 512
TOPK = 8
DC = D // 128
HC = HID // 128
CSHIFT = 12.0
EPS_LN = 1e-5
EPS_L2 = 1e-12
BIG = 100.0

CONSTS = [-CSHIFT, 1.0 / D, -1.0, -0.5, 0.5, EPS_LN, EPS_L2, BIG, float(D), 1.0, 0.0, 1.0 / (D * D), -1.0 / D]


class Prog:
    ENGS = ("tensor", "vector", "scalar", "gpsimd", "sync")
    DMAQ = ("sync", "gpsimd")
    NSLOT = 8

    def __init__(self, nc):
        self.nc = nc
        self.chans = list(self.ENGS) + [f"{q}.d{i}" for q in self.DMAQ
                                        for i in range(self.NSLOT)]
        self.sem = {c: nc.alloc_semaphore(f"qs_{c}") for c in self.chans}
        self.cnt = {c: 0 for c in self.chans}
        self.clock = {e: {c: 0 for c in self.chans} for e in self.ENGS}
        self.opclock = {}
        self.w = {}
        self.r = {}
        self.ops = {e: [] for e in self.ENGS}
        self.dmarr = {q: 0 for q in self.DMAQ}
        self.nwaits = 0

    def _need(self, eng, dep):
        d_ch, d_cnt = dep
        if d_ch == eng == "tensor":
            return
        if self.clock[eng][d_ch] >= d_cnt:
            return
        sem = self.sem[d_ch]
        self.ops[eng].append(lambda g, s=sem, c=d_cnt: g.wait_ge(s, c))
        self.nwaits += 1
        pc = self.opclock.get((d_ch, d_cnt))
        if pc:
            ce = self.clock[eng]
            for k, v in pc.items():
                if v > ce[k]:
                    ce[k] = v
        if d_cnt > self.clock[eng][d_ch]:
            self.clock[eng][d_ch] = d_cnt

    def op(self, eng, fn, reads=(), writes=(), dma=False):
        for k in reads:
            if k in self.w:
                self._need(eng, self.w[k])
        for k in writes:
            if k in self.w:
                self._need(eng, self.w[k])
            for dep in self.r.get(k, ()):
                self._need(eng, dep)
        if dma:
            slot = self.dmarr[eng]
            self.dmarr[eng] = (slot + 1) % self.NSLOT
            ch = f"{eng}.d{slot}"
            self._need(eng, (ch, self.cnt[ch]))
            self.cnt[ch] += 16
            cnt = self.cnt[ch]
            sem = self.sem[ch]

            def run(g, fn=fn, sem=sem):
                fn(g).then_inc(sem, 16)
            self.ops[eng].append(run)
            oc = dict(self.clock[eng])
            oc[ch] = cnt
            self.opclock[(ch, cnt)] = oc
            tok = (ch, cnt)
        else:
            self.cnt[eng] += 1
            cnt = self.cnt[eng]
            sem = self.sem[eng]

            def run(g, fn=fn, sem=sem):
                fn(g).then_inc(sem, 1)
            self.ops[eng].append(run)
            oc = dict(self.clock[eng])
            oc[eng] = cnt
            self.opclock[(eng, cnt)] = oc
            tok = (eng, cnt)
        for k in writes:
            self.w[k] = tok
            self.r[k] = []
        for k in reads:
            self.r.setdefault(k, []).append(tok)
        return tok

    def fence(self, eng, keys):
        for k in keys:
            if k in self.w:
                self._need(eng, self.w[k])
            for dep in self.r.get(k, ()):
                self._need(eng, dep)

    def render(self):
        with self.nc.Block() as blk:
            for e in self.ENGS:
                ops = self.ops[e]
                if ops:
                    getattr(blk, e)(lambda g, ops=ops: [f(g) for f in ops])


class Pump:

    def __init__(self, P):
        self.P = P
        self.q = []

    def add(self, eng, fn, reads=(), writes=(), dma=False):
        self.q.append((eng, fn, reads, writes, dma))

    def pump(self, n):
        for _ in range(min(n, len(self.q))):
            eng, fn, reads, writes, dma = self.q.pop(0)
            self.P.op(eng, fn, reads=reads, writes=writes, dma=dma)

    def drain(self):
        self.pump(len(self.q))


def build(nt_core=2048, s_batch=S, taps=False, ag=True, ag_groups=None):
    assert nt_core % 512 == 0 and s_batch % 512 == 0
    NT = nt_core // 128
    NRANK = s_batch // nt_core
    if ag_groups is None:
        ag_groups = [[0, 1, 2, 3], [4, 5, 6, 7]] if (NRANK == 4 and NCORES == 8) \
            else [list(range(NRANK))]
    NG = (nt_core // 512) if ag else (s_batch // 512)
    NGQ = nt_core // 512
    KCHUNK = min(1024, nt_core if ag else s_batch)
    NKC = s_batch // KCHUNK
    CPR = nt_core // KCHUNK if ag else 0
    SPLIT_AG = ag and CPR == 2
    if SPLIT_AG:
        KC_ORDER = [k for k in range(NKC) if k % 2 == 0] + \
                   [k for k in range(NKC) if k % 2 == 1]
    else:
        KC_ORDER = list(range(NKC))
    KS = KCHUNK // 128
    QG = 512
    NQG = nt_core // QG
    QT_PER_G = QG // 128
    MCOL = s_batch // 128
    NGT = nt_core // 512

    nc = bass.Bass("TRN2", target_bir_lowering=False)

    xrot = nc.dram_tensor("xrot", [nt_core if ag else s_batch, D], F32, kind="ExternalInput")
    wq = nc.dram_tensor("wq", [D, D], F32, kind="ExternalInput")
    wk = nc.dram_tensor("wk", [D, D], F32, kind="ExternalInput")
    wv = nc.dram_tensor("wv", [D, D], F32, kind="ExternalInput")
    wo = nc.dram_tensor("wo", [D, D], F32, kind="ExternalInput")
    bqkvo = nc.dram_tensor("bqkvo", [128, 4 * DC], F32, kind="ExternalInput")
    borow = nc.dram_tensor("borow", [1, D], F32, kind="ExternalInput")
    maskpp = nc.dram_tensor("maskpp", [128, MCOL], F32, kind="ExternalInput")
    gatew = nc.dram_tensor("gatew", [E_EXP, D], F32, kind="ExternalInput")
    wgate = nc.dram_tensor("wgate", [E_EXP, D, HID], F32, kind="ExternalInput")
    wup = nc.dram_tensor("wup", [E_EXP, D, HID], F32, kind="ExternalInput")
    wdown = nc.dram_tensor("wdown", [E_EXP, HID, D], F32, kind="ExternalInput")
    swg = nc.dram_tensor("swg", [D, HID], F32, kind="ExternalInput")
    swu = nc.dram_tensor("swu", [D, HID], F32, kind="ExternalInput")
    swd = nc.dram_tensor("swd", [HID, D], F32, kind="ExternalInput")
    identf = nc.dram_tensor("identf", [128, 128], F32, kind="ExternalInput")
    identb = nc.dram_tensor("identb", [128, 128], BF16, kind="ExternalInput")
    onesrow = nc.dram_tensor("onesrow", [1, 128], F32, kind="ExternalInput")
    onescol = nc.dram_tensor("onescol", [128, 1], BF16, kind="ExternalInput")
    constsin = nc.dram_tensor("constsin", [128, len(CONSTS)], F32, kind="ExternalInput")
    out = nc.dram_tensor("out", [nt_core, D], F32, kind="ExternalOutput")

    kscr = nc.dram_tensor("kscr", [D, s_batch], BF16, kind="Internal")
    vscr = nc.dram_tensor("vscr", [s_batch, D], BF16, kind="Internal")
    kag_in = nc.dram_tensor("kag_in", [D, nt_core], BF16, kind="Internal")
    kag_inh = [nc.dram_tensor(f"kag_inh{h}", [D, nt_core // 2], BF16, kind="Internal")
               for h in range(2)]
    vag_in = nc.dram_tensor("vag_in", [nt_core, D], BF16, kind="Internal")
    kag_out = [nc.dram_tensor(f"kag_out{h}", [NRANK * D, nt_core // 2], BF16, kind="Internal")
               for h in range(2)]
    vag_out = [nc.dram_tensor(f"vag_out{h}", [s_batch // 2, D], BF16, kind="Internal")
               for h in range(2)]
    kag_outF = nc.dram_tensor("kag_outF", [NRANK * D, nt_core], BF16, kind="Internal")
    vag_outF = nc.dram_tensor("vag_outF", [s_batch, D], BF16, kind="Internal")
    x1scr = nc.dram_tensor("x1scr", [nt_core, D], F32, kind="Internal")

    taps_t = {}
    if taps:
        taps_t["t_x1"] = nc.dram_tensor("t_x1", [nt_core, D], F32, kind="ExternalOutput")
        taps_t["t_cb"] = nc.dram_tensor("t_cb", [128, (nt_core // 128) * E_EXP], F32, kind="ExternalOutput")
        taps_t["t_ffn"] = nc.dram_tensor("t_ffn", [nt_core, D], F32, kind="ExternalOutput")
        taps_t["t_att"] = nc.dram_tensor("t_att", [D, s_batch], BF16, kind="ExternalOutput")

    SB_BASE = 16512
    ptr = [SB_BASE]

    def sba(name, shape, dtype, at=None):
        per = int(np.prod(shape[1:])) * mybir.dt.size(dtype)
        per = (per + 31) // 32 * 32
        if at is None:
            at = ptr[0]
            ptr[0] = at + per
        else:
            ptr[0] = max(ptr[0], at + per)
        return nc.alloc_sbuf_tensor_at(name, list(shape), dtype, offset=at)

    c_ident = sba("c_ident", [128, 128], F32)
    c_identb = sba("c_identb", [128, 128], BF16)
    c_ones = sba("c_ones", [1, 128], F32)
    c_onescol = sba("c_onescol", [128, 1], BF16)
    c_mask = sba("c_mask", [128, MCOL], F32)
    c_bias = sba("c_bias", [128, 4 * DC], F32)
    c_const = sba("c_const", [128, len(CONSTS)], F32)
    c_borow = sba("c_borow", [1, D], F32)
    c_bobc = sba("c_bobc", [128, D], F32)
    c_gnt = sba("c_gnt", [128, DC * E_EXP], BF16)
    c_gwbf = sba("c_gwbf", [E_EXP, D], BF16)
    c_lacc = sba("c_lacc", [128, NT], F32)
    c_rl = sba("c_rl", [128, NT], F32)
    c_st = sba("c_st", [128, 160], F32)
    c_comb = sba("c_comb", [128, (nt_core // 128) * E_EXP], F32)
    c_g4 = [sba(f"c_g4_{i}", [128, 256], F32) for i in range(2)]
    c_bvb = sba("c_bvb", [128, DC], BF16)
    for i, v in enumerate(CONSTS):
        nc.const_aps.register(F32, float(v), c_const[:, i:i + 1])

    offA = ptr[0]
    qt_sb = sba("qt_sb", [128, DC, nt_core], BF16, at=offA)
    ln2t = sba("ln2t", [128, DC, nt_core], BF16, at=offA)

    offB = ptr[0]
    WTQ = sba("wtq", [128, DC, D], BF16, at=offB)
    WTK = sba("wtk", [128, DC, D], BF16, at=offB + DC * D * 2)
    WVOT = sba("wvot", [128, DC, D], BF16, at=offB + 2 * DC * D * 2)
    oaccA = sba("oaccA", [128, NT * D], F32, at=offB)
    out_acc = [oaccA[:, t * D:(t + 1) * D] for t in range(NT)]
    routed = [sba(f"routed{t}", [128, D], F32, at=offB + t * D * 4) for t in range(NT)]
    ptr[0] = offB + max(NT * D * 4, 3 * DC * D * 2)

    offC = ptr[0]
    wn_tmp = sba("wn_tmp", [128, DC, D], BF16, at=offC)
    wto = sba("wto", [128, DC, D], BF16, at=offC + 16384)
    wvn = sba("wvn", [128, DC, D], BF16, at=offC + 32768)
    gw_tmp = sba("gw_tmp", [E_EXP, D], F32, at=offC + 49152)
    gwsq = sba("gwsq", [E_EXP, D], F32, at=offC + 53248)
    if ag:
        xg = [sba("xg0", [128, 4 * D], F32, at=offC)]
        ln1g = [sba(f"ln1g{b}", [128, 4 * D], BF16, at=offC + 16384 + b * 8192) for b in range(2)]
        ln1t = [sba(f"ln1t{b}", [128, DC, 512], BF16, at=offC + 32768 + b * 8192)
                for b in range(4)]
    else:
        xg = [sba(f"xg{b}", [128, 4 * D], F32, at=offC + b * 16384) for b in range(2)]
        ln1g = [sba(f"ln1g{b}", [128, 4 * D], BF16, at=offC + 32768 + b * 8192) for b in range(2)]
        ln1t = [sba(f"ln1t{b}", [128, DC, 512], BF16, at=offC + 49152 + b * 8192) for b in range(2)]
    NXG = len(xg)
    NLT = len(ln1t)
    kbuf = [sba(f"kbuf{b}", [128, DC, KCHUNK], BF16, at=offC + b * 16384) for b in range(2)]
    vbuf = [sba(f"vbuf{b}", [128, KS, D], BF16, at=offC + 32768 + b * 16384) for b in range(2)]
    p3_af = [sba(f"p3_af{i}", [128, D], F32, at=offC + i * 4096) for i in range(2)]
    p3_sq = [sba(f"p3_sq{i}", [128, D], F32, at=offC + 8192 + i * 4096) for i in range(2)]
    p3_ln = [sba(f"p3_ln{i}", [128, D], F32, at=offC + 32768 + i * 4096) for i in range(2)]
    p3_xr = [sba(f"p3_xr{i}", [128, D], F32, at=offC + 40960 + i * 4096) for i in range(2)] \
        + [sba(f"p3_xr{i + 2}", [128, D], F32, at=offC + i * 4096) for i in range(2)]
    wgbsh = sba("wgbsh", [128, DC, HID], BF16, at=offC + 16384)
    wubsh = sba("wubsh", [128, DC, HID], BF16, at=offC + 24576)
    wdbsh = sba("wdbsh", [128, HC, D], BF16, at=offC + 49152)
    wgb = [sba(f"wgb{b}", [128, DC, HID], BF16, at=offC + b * 8192) for b in range(2)]
    wub = [sba(f"wub{b}", [128, DC, HID], BF16, at=offC + 16384 + b * 8192) for b in range(2)]
    wdb = [sba("wdb", [128, HC, D], BF16, at=offC + 32768),
           sba("wdb1", [128, HC, D], BF16, at=offC + 49152)]
    ptr[0] = offC + 65536

    offD = ptr[0]
    kst = [sba("kst", [128, DC, 512], BF16, at=offD),
           sba("kst1", [128, DC, 512], BF16, at=offD + 16384)]
    vst = sba("vst", [128, 4, D], BF16, at=offD + 8192)
    pt = [[sba(f"pt{b}_{k}", [128, QG], BF16, at=offD + (b * KS + k) * QG * 2)
           for k in range(KS)] for b in range(2)]
    p3_ln2 = [sba(f"p3_ln2_{i}", [128, D], BF16, at=offD + 16384 + i * 2048) for i in range(4)]
    actt = [sba(f"actt{k}", [128, nt_core], BF16, at=offD + k * 4096) for k in range(4)]
    actt += [sba(f"actt{k + 4}", [128, nt_core], BF16, at=offC + 57344 + k * 4096)
             for k in range(2)]
    ew = [sba("ew0", [128, 512], F32, at=offD + 24576),
          sba("ew1", [128, 512], F32, at=offD + 26624)]
    p1_sq = [sba("p1_sq0", [128, D], F32, at=offD + 28672),
             sba("p1_sq1", [128, D], F32, at=offD + 24576)]
    ptr[0] = offD + 32768

    assert ptr[0] <= 229300, f"SBUF over budget: {ptr[0]}"

    psA = [nc.alloc_psum_tensor(f"psA{i}", [128, 1024], F32) for i in range(2)]
    psB = [nc.alloc_psum_tensor(f"psB{i}", [128, 512], F32) for i in range(3)]
    ps7b = nc.alloc_psum_tensor("ps7b", [128, 1024], BF16)
    psB2b = psB[2][:, :].bitcast(BF16)
    tbank = [(ps7b[:, :], "ps7b"), (psB2b, "psB2")]

    P = Prog(nc)

    def dma(q, dst, src, reads=(), writes=()):
        return P.op(q, lambda g: g.dma_start(dst, src), reads=reads, writes=writes, dma=True)

    dma("sync", c_ident[:, :], identf[:, :], writes=["ident"])
    dma("sync", c_identb[:, :], identb[:, :], writes=["identb"])
    dma("sync", c_ones[:, :], onesrow[:, :], writes=["ones"])
    dma("sync", c_onescol[:, :], onescol[:, :], writes=["onescol"])
    dma("sync", c_mask[:, :], maskpp[:, :], writes=["mask"])
    dma("sync", c_bias[:, :], bqkvo[:, :], writes=["bias"])
    dma("sync", c_const[:, :], constsin[:, :], writes=["const"])
    dma("sync", c_borow[:, :], borow[:, :], writes=["borow"])
    dma("sync", gw_tmp[:, :], gatew[:, :], writes=["gwtmp"])
    for _e in Prog.ENGS:
        P.fence(_e, ["const", "ident", "identb", "ones", "onescol", "mask", "bias"])

    HR = DC // 2

    QR = DC // 4

    def cast_wt(wdram, stage, tag, prev):
        for h in range(4):
            extra = [f"wn_{prev}h{h}"] if prev else []
            dma("gpsimd", stage[:, h * QR:(h + 1) * QR, :],
                wdram[h * QR * 128:(h + 1) * QR * 128, :].rearrange(
                    "(r p) d -> p r d", p=128),
                writes=[f"wn_{tag}h{h}"] + extra)

    def trans_wt(stage, wt_sb, tag):
        for r in range(DC):
            bk, bkey = tbank[r % 2]
            h = r // QR
            P.op("tensor",
                 lambda t, r=r, bk=bk, stage=stage: [t.transpose(
                     bk[:, dc * 128:(dc + 1) * 128],
                     stage[:, r, dc * 128:(dc + 1) * 128],
                     c_identb[:, :]) for dc in range(DC)][-1],
                 reads=[f"wn_{tag}h{h}", "identb"], writes=[bkey])
            P.op("vector",
                 lambda v, r=r, wt_sb=wt_sb, bk=bk: v.tensor_copy(
                     wt_sb[:, :, r * 128:(r + 1) * 128],
                     bk.rearrange("p (dc j) -> p dc j", dc=DC)),
                 reads=[bkey], writes=[f"wt_{tag}.r{r}"])

    cast_wt(wq, wn_tmp, "q", None)
    cast_wt(wv, wvn, "v", None)
    trans_wt(wn_tmp, WTQ, "q")
    cast_wt(wk, wn_tmp, "k", "q")
    trans_wt(wn_tmp, WTK, "k")
    cast_wt(wo, wn_tmp, "o", "k")
    trans_wt(wn_tmp, wto, "o")
    wvn_keys = [f"wn_vh{h}" for h in range(4)]

    wto_keys = [f"wt_o.r{r}" for r in range(DC)]
    wtq_keys = [f"wt_q.r{r}" for r in range(DC)]
    wtk_keys = [f"wt_k.r{r}" for r in range(DC)]
    for db in range(DC):
        for j in range(2):
            pb = (db * 2 + j) % 2
            def mmwvo(t, db=db, j=j, pb=pb):
                for cc in range(DC):
                    inst = t.matmul(psB[pb][:, :],
                                    wvn[:, cc, db * 128:(db + 1) * 128],
                                    wto[:, cc, j * 512:(j + 1) * 512],
                                    start=(cc == 0), stop=(cc == DC - 1))
                return inst
            P.op("tensor", mmwvo, reads=wvn_keys + wto_keys, writes=[f"psB{pb}"])
            P.op("vector", lambda v, db=db, j=j, pb=pb: v.tensor_copy(
                WVOT[:, db, j * 512:(j + 1) * 512], psB[pb][:, :]),
                reads=[f"psB{pb}"], writes=[f"wvot.{db}.{j}"])
    wvot_keys = [f"wvot.{db}.{j}" for db in range(DC) for j in range(2)]

    P.op("scalar", lambda s: s.copy(c_bvb[:, :], c_bias[:, 2 * DC:3 * DC]),
         reads=["bias"], writes=["bvb"])
    for j in range(2):
        def mmbo(t, j=j):
            for cc in range(DC):
                inst = t.matmul(psA[1][0:1, j * 512:(j + 1) * 512],
                                c_bvb[:, cc:cc + 1],
                                wto[:, cc, j * 512:(j + 1) * 512],
                                start=(cc == 0), stop=(cc == DC - 1))
            return inst
        P.op("tensor", mmbo, reads=["bvb"] + wto_keys, writes=[f"psA1.j{j}"])
    P.op("vector", lambda v: v.tensor_add(c_borow[0:1, :], c_borow[0:1, :], psA[1][0:1, 0:D]),
         reads=["psA1.j0", "psA1.j1", "borow"], writes=["borow"])
    for j in range(2):
        P.op("tensor", lambda t, j=j: t.matmul(psA[0][:, j * 512:(j + 1) * 512],
                                               c_ones[0:1, :],
                                               c_borow[0:1, j * 512:(j + 1) * 512],
                                               start=True, stop=True),
             reads=["borow", "ones"], writes=[f"psA0.j{j}"])
    P.op("vector", lambda v: v.tensor_copy(c_bobc[:, :], psA[0][:, 0:D]),
         reads=["psA0.j0", "psA0.j1"], writes=["bobc"])

    P.op("scalar", lambda s: s.activation(gwsq[:, :], gw_tmp[:, :], AF.Square,
                                          accum_out=c_st[0:E_EXP, 0:1]),
         reads=["gwtmp"], writes=["gwsq", "g.ss"])
    P.op("scalar", lambda s: s.activation(c_st[0:E_EXP, 1:2], c_st[0:E_EXP, 0:1], AF.Ln),
         reads=["g.ss"], writes=["g.ln"])
    P.op("scalar", lambda s: s.activation(c_st[0:E_EXP, 1:2], c_st[0:E_EXP, 1:2], AF.Exp, scale=-0.5),
         reads=["g.ln"], writes=["g.ln"])
    P.op("vector", lambda v: v.tensor_scalar(gwsq[:, :], gw_tmp[:, :], c_st[0:E_EXP, 1:2],
                                             None, ALU.mult),
         reads=["gwtmp", "g.ln"], writes=["gwsq"])
    P.op("scalar", lambda s: s.copy(c_gwbf[:, :], gwsq[:, :]),
         reads=["gwsq"], writes=["gwbf"])
    P.op("tensor", lambda t: [t.transpose(ps7b[:, dc * 16:(dc + 1) * 16],
                                          c_gwbf[0:E_EXP, dc * 128:(dc + 1) * 128],
                                          c_identb[0:E_EXP, 0:E_EXP]) for dc in range(DC)][-1],
         reads=["gwbf", "identb"], writes=["ps7b"])
    P.op("vector", lambda v: v.tensor_copy(c_gnt[:, :], ps7b[:, 0:DC * E_EXP]),
         reads=["ps7b"], writes=["gnt"])

    BQ, BK = 0, DC

    def ln_stats_ops(pump_or_P, xap, sqbuf, sqkey, slot, xkey, sum_on_scalar=False):
        co = 8 + 4 * slot
        s_sum = c_st[:, co:co + 1]
        s_var = c_st[:, co + 1:co + 2]
        s_r = c_st[:, co + 2:co + 3]
        s_b = c_st[:, co + 3:co + 4]
        kS, kV, kR, kB = (f"t{co}", f"t{co + 1}", f"t{co + 2}", f"t{co + 3}")
        if isinstance(pump_or_P, Pump):
            A = pump_or_P.add
        else:
            A = lambda eng, fn, reads=(), writes=(): pump_or_P.op(eng, fn, reads=reads, writes=writes)
        if sum_on_scalar:
            A("scalar", lambda s: s.activation(sqbuf, xap, AF.Identity,
                                               accum_out=s_sum),
              reads=[xkey], writes=[sqkey, kS])
        else:
            A("vector", lambda v: v.reduce_sum(s_sum, xap, axis=AX),
              reads=[xkey], writes=[kS])
        A("scalar", lambda s: s.activation(sqbuf, xap, AF.Square, accum_out=s_var),
          reads=[xkey], writes=[sqkey, kV])
        A("vector", lambda v: v.tensor_scalar(s_b, s_sum, s_sum, 1.0 / (D * D),
                                              ALU.mult, ALU.mult),
          reads=[kS], writes=[kB])
        A("vector", lambda v: v.scalar_tensor_tensor(s_var, s_var, 1.0 / D, s_b,
                                                     ALU.mult, ALU.subtract),
          reads=[kV, kB], writes=[kV])
        A("scalar", lambda s: s.activation(s_r, s_var, AF.Ln, bias=EPS_LN),
          reads=[kV], writes=[kR])
        A("scalar", lambda s: s.activation(s_r, s_r, AF.Exp, scale=-0.5),
          reads=[kR], writes=[kR])
        A("vector", lambda v: v.scalar_tensor_tensor(s_b, s_sum, -1.0 / D, s_r,
                                                     ALU.mult, ALU.mult),
          reads=[kS, kR], writes=[kB])
        return s_r, s_b, s_var, kR, kB, kV

    pump = Pump(P)

    def p1_load(g):
        b = g % NXG
        for n in range(4):
            if g == 0:
                xg_alias = [f"wn_oh{n}"]
            elif g == 1 and not ag:
                xg_alias = [f"wt_o.r{2 * n}", f"wt_o.r{2 * n + 1}",
                            "psA1.j0", "psA1.j1"]
            else:
                xg_alias = []
            dma("gpsimd", xg[b][:, n * D:(n + 1) * D],
                xrot[(g * 4 + n) * 128:(g * 4 + n + 1) * 128, :],
                writes=[f"xg{b}n{n}"] + xg_alias)

    def p1_chain(g):
        b = g % NXG
        bl = g % len(ln1g)
        for n in range(4):
            xap = xg[b][:, n * D:(n + 1) * D]
            sq = p1_sq[n % 2]
            r, bb, _, kR, kB, _ = ln_stats_ops(pump, xap, sq[:, :], f"p1sq{n % 2}",
                                               (g * 4 + n) % 4, f"xg{b}n{n}")
            if ag:
                lg_alias = wto_keys + ["psA1.j0", "psA1.j1"] if (g < 2 and n == 0) else []
            else:
                lg_alias = wvn_keys if (g < 2 and n == 0) else []
            pump.add("scalar", lambda s, n=n, bl=bl, r=r, bb=bb, xap=xap: s.activation(
                ln1g[bl][:, n * D:(n + 1) * D], xap, AF.Identity, bias=bb, scale=r),
                reads=[f"xg{b}n{n}", kR, kB], writes=[f"ln1g{bl}.n{n}"] + lg_alias)

    def p1_transpose(g):
        b = g % len(ln1g)
        lb = g % NLT
        for n in range(4):
            bk, bkey = tbank[n % 2]
            if ag:
                lt_alias = (wvn_keys if (g <= 1 and n == 0) else []) \
                    + (["gwtmp", "gwsq"] if (g == 2 and n == 0) else [])
            else:
                lt_alias = ["gwtmp", "gwsq"] if (g == 0 and n == 0) else []
            P.op("tensor", lambda t, n=n, b=b, bk=bk: [t.transpose(
                bk[:, dc * 128:(dc + 1) * 128],
                ln1g[b][:, n * D + dc * 128:n * D + (dc + 1) * 128],
                c_identb[:, :]) for dc in range(DC)][-1],
                reads=[f"ln1g{b}.n{n}", "identb"], writes=[bkey])
            P.op("vector", lambda v, n=n, lb=lb, bk=bk: v.tensor_copy(
                ln1t[lb][:, :, n * 128:(n + 1) * 128],
                bk.rearrange("p (dc j) -> p dc j", dc=DC)),
                reads=[bkey], writes=[f"ln1t{lb}.n{n}"] + lt_alias)

    def p1_k(g):
        lb = g % NLT
        lt_keys = [f"ln1t{lb}.n{n}" for n in range(4)]
        if ag and NG >= 2:
            HW2 = nt_core // 2
            h = (g * 512) // HW2
            kdst_ap = kag_inh[h][:, g * 512 - h * HW2:(g + 1) * 512 - h * HW2]
        elif ag:
            kdst_ap = kag_in[:, g * 512:(g + 1) * 512]
        else:
            kdst_ap = kscr[:, g * 512:(g + 1) * 512]
        kb = g % 2
        for ic in range(DC):
            pb = ic % 2
            def mmk(t, ic=ic, lb=lb, pb=pb):
                for dc in range(DC):
                    inst = t.matmul(psB[pb][:, :], WTK[:, dc, ic * 128:(ic + 1) * 128],
                                    ln1t[lb][:, dc, :], start=(dc == 0), stop=(dc == DC - 1))
                return inst
            P.op("tensor", mmk, reads=lt_keys + wtk_keys, writes=[f"psB{pb}"])
            P.op("scalar", lambda s, ic=ic, pb=pb, kb=kb: s.activation(
                kst[kb][:, ic, :], psB[pb][:, :], AF.Identity,
                bias=c_bias[:, BK + ic:BK + ic + 1], scale=1.0),
                reads=[f"psB{pb}", "bias"], writes=[f"kst{kb}.{ic}"])
            pump.pump(2)
        dma("sync", kdst_ap.rearrange("(ic p) k -> p ic k", p=128),
            kst[kb][:, :, :], reads=[f"kst{kb}.{ic}" for ic in range(DC)],
            writes=[f"Kscr.g{g}"])

    def p1_v(g):
        lb = g % NLT
        vdst = vag_in if ag else vscr
        for n in range(4):
            for j in range(2):
                pb = (n * 2 + j) % 2
                def mmv(t, n=n, j=j, lb=lb, pb=pb):
                    for dc in range(DC):
                        inst = t.matmul(psB[pb][:, :],
                                        ln1t[lb][:, dc, n * 128:(n + 1) * 128],
                                        WVOT[:, dc, j * 512:(j + 1) * 512],
                                        start=(dc == 0), stop=(dc == DC - 1))
                    return inst
                P.op("tensor", mmv, reads=[f"ln1t{lb}.n{n}"] + wvot_keys, writes=[f"psB{pb}"])
                P.op("scalar", lambda s, n=n, j=j, pb=pb: s.copy(
                    vst[:, n, j * 512:(j + 1) * 512], psB[pb][:, :]),
                    reads=[f"psB{pb}"], writes=[f"vst.{n}.{j}"])
                pump.pump(2)
        dma("sync", vdst[g * 512:(g + 1) * 512, :].rearrange("(n p) d -> p n d", p=128),
            vst[:, :, :], reads=[f"vst.{n}.{j}" for n in range(4) for j in range(2)],
            writes=[f"Vscr.g{g}"])

    def p1_q(g):
        lb = g % NLT
        lt_keys = [f"ln1t{lb}.n{n}" for n in range(4)]
        for ic in range(DC):
            pb = ic % 2
            def mmq(t, ic=ic, lb=lb, pb=pb):
                for dc in range(DC):
                    inst = t.matmul(psB[pb][:, :], WTQ[:, dc, ic * 128:(ic + 1) * 128],
                                    ln1t[lb][:, dc, :], start=(dc == 0), stop=(dc == DC - 1))
                return inst
            P.op("tensor", mmq, reads=lt_keys + wtq_keys, writes=[f"psB{pb}"])
            P.op("scalar", lambda s, ic=ic, g=g, pb=pb: s.activation(
                qt_sb[:, ic, g * 512:(g + 1) * 512], psB[pb][:, :], AF.Identity,
                bias=c_bias[:, BQ + ic:BQ + ic + 1], scale=1.0),
                reads=[f"psB{pb}", "bias"], writes=[f"qt.g{g}.i{ic}"])
            pump.pump(2)

    def p1_mm(g):
        p1_k(g)
        p1_v(g)
        if not ag and g < NGQ:
            p1_q(g)

    def kv_load(kc, b, first=False):
        galias, valias = [], []
        if first and ag:
            galias = [f"xg0n{n}" for n in range(4)] if b == 0 else [f"ln1g{bb}.n{n}" for bb in range(2) for n in range(4)]
            valias = ([f"ln1t{bb}.n{n}" for bb in (0, 1) for n in range(4)] if b == 0 else
                      [f"ln1t{bb}.n{n}" for bb in (2, 3) for n in range(4)] + ["gwtmp", "gwsq"])
        elif first:
            galias = [f"xg{b}n{n}" for n in range(4)]
            valias = ([f"ln1g{bb}.n{n}" for bb in range(2) for n in range(4)] if b == 0 else
                      [f"ln1t{bb}.n{n}" for bb in range(NLT) for n in range(4)] + ["gwtmp", "gwsq"])
        if ag and NG >= 2:
            HW2 = nt_core // 2
            nparts = max(1, KCHUNK // HW2)
            plen = KCHUNK // nparts
            for pi in range(nparts):
                off_g = kc * KCHUNK + pi * plen
                rank = off_g // nt_core
                off = off_g % nt_core
                h = off // HW2
                hoff = off % HW2
                ksrc = kag_out[h][rank * D:(rank + 1) * D, hoff:hoff + plen]
                vsrc = vag_out[h][rank * HW2 + hoff:rank * HW2 + hoff + plen, :]
                dma("sync", kbuf[b][:, :, pi * plen:(pi + 1) * plen],
                    ksrc.rearrange("(ic p) k -> p ic k", p=128),
                    reads=[f"kago{h}"],
                    writes=[f"kbuf{b}"] + (galias if pi == 0 else []))
            for pi in range(nparts):
                off_g = kc * KCHUNK + pi * plen
                rank = off_g // nt_core
                off = off_g % nt_core
                h = off // HW2
                hoff = off % HW2
                vsrc = vag_out[h][rank * HW2 + hoff:rank * HW2 + hoff + plen, :]
                dma("sync", vbuf[b][:, pi * (plen // 128):(pi + 1) * (plen // 128), :],
                    vsrc.rearrange("(n p) d -> p n d", p=128),
                    reads=[f"vago{h}"],
                    writes=[f"vbuf{b}"] + (valias if pi == 0 else []))
            return
        elif ag:
            rank = (kc * KCHUNK) // nt_core
            off = (kc * KCHUNK) % nt_core
            ksrc = kag_outF[rank * D:(rank + 1) * D, off:off + KCHUNK]
            vsrc = vag_outF[kc * KCHUNK:(kc + 1) * KCHUNK, :]
            krd, vrd = ["kago0"], ["vago0"]
        else:
            ksrc = kscr[:, kc * KCHUNK:(kc + 1) * KCHUNK]
            vsrc = vscr[kc * KCHUNK:(kc + 1) * KCHUNK, :]
            krd = [f"Kscr.g{gg}" for gg in range(kc * KCHUNK // 512, (kc + 1) * KCHUNK // 512)]
            vrd = [f"Vscr.g{gg}" for gg in range(kc * KCHUNK // 512, (kc + 1) * KCHUNK // 512)]
        dma("sync", kbuf[b][:, :, :], ksrc.rearrange("(ic p) k -> p ic k", p=128),
            reads=krd, writes=[f"kbuf{b}"] + galias)
        dma("sync", vbuf[b][:, :, :], vsrc.rearrange("(n p) d -> p n d", p=128),
            reads=vrd, writes=[f"vbuf{b}"] + valias)

    p1_load(0)
    p1_chain(0)
    pump.drain()
    kv_early = [False]
    if ag:
        HW = nt_core // 2

        def ag_k(h):
            P.op("gpsimd", lambda g, h=h: g.collective_compute(
                "AllGather", ALU.bypass, replica_groups=ag_groups,
                ins=[kag_inh[h].ap().opt()],
                outs=[kag_out[h].ap().opt()]),
                reads=[f"Kscr.g{g}" for g in range(h * NG // 2, (h + 1) * NG // 2)]
                if SPLIT_AG else [f"Kscr.g{g}" for g in range(NG)],
                writes=[f"kago{h}"])

        def ag_v(h):
            P.op("gpsimd", lambda g, h=h: g.collective_compute(
                "AllGather", ALU.bypass, replica_groups=ag_groups,
                ins=[vag_in[h * HW:(h + 1) * HW, :].opt()],
                outs=[vag_out[h].ap().opt()]),
                reads=[f"Vscr.g{g}" for g in range(h * NG // 2, (h + 1) * NG // 2)]
                if SPLIT_AG else [f"Vscr.g{g}" for g in range(NG)],
                writes=[f"vago{h}"])

        if NG < 2:
            def ag_k(h, _ag_k=ag_k):
                if h == 0:
                    P.op("gpsimd", lambda g: g.collective_compute(
                        "AllGather", ALU.bypass, replica_groups=ag_groups,
                        ins=[kag_in.ap().opt()], outs=[kag_outF.ap().opt()]),
                        reads=[f"Kscr.g{g}" for g in range(NG)], writes=["kago0", "kago1"])

            def ag_v(h, _ag_v=ag_v):
                if h == 0:
                    P.op("gpsimd", lambda g: g.collective_compute(
                        "AllGather", ALU.bypass, replica_groups=ag_groups,
                        ins=[vag_in.ap().opt()], outs=[vag_outF.ap().opt()]),
                        reads=[f"Vscr.g{g}" for g in range(NG)], writes=["vago0", "vago1"])
        if SPLIT_AG:
            for g in range(NG // 2):
                if g + 1 < NG:
                    p1_load(g + 1)
                    p1_chain(g + 1)
                p1_transpose(g)
                p1_k(g)
                pump.drain()
            ag_k(0)
            for g in range(NG // 2):
                if NG // 2 + g + 1 < NG:
                    p1_load(NG // 2 + g + 1)
                    p1_chain(NG // 2 + g + 1)
                p1_v(g)
                pump.drain()
            ag_v(0)
            for g in range(NG // 2, NG):
                p1_transpose(g)
                p1_k(g)
                pump.drain()
            ag_k(1)
            for g in range(NG // 2, NG):
                p1_v(g)
                pump.drain()
            ag_v(1)
        else:
            for g in range(NG):
                if g + 1 < NG:
                    p1_load(g + 1)
                    p1_chain(g + 1)
                p1_transpose(g)
                p1_k(g)
                pump.drain()
            ag_k(0)
            ag_k(1)
            for g in range(NG):
                p1_v(g)
            ag_v(0)
            ag_v(1)
        for g in range(NG):
            p1_q(g)
    else:
        for g in range(NG):
            if g + 1 < NG:
                p1_load(g + 1)
                p1_chain(g + 1)
            p1_transpose(g)
            if g == NG - 1 and NG >= 4 and NKC > 1:
                kv_load(0, 0, first=True)
                kv_early[0] = True
            p1_mm(g)
            pump.drain()

    if taps:
        dma("sync", taps_t["t_att"][:, (NG - 1) * 512:NG * 512].rearrange("(ic p) k -> p ic k", p=128),
            kst[(NG - 1) % 2][:, :, :],
            reads=[f"kst{(NG - 1) % 2}.{ic}" for ic in range(DC)], writes=["tapatt"])

    CST_GRP = 64

    def st4(gp, off):
        return c_st[:, 64 + gp * 48 + off:64 + gp * 48 + off + 4]

    def st1(gp, off, q):
        return c_st[:, 64 + gp * 48 + off + q:64 + gp * 48 + off + q + 1]

    def bc16(gp, off):
        return bass.AP(c_st, 64 + gp * 48 + off, [[160, 128], [1, 4], [0, 16]])

    def oacc3d(gi):
        return oaccA[:, 4 * gi * D:(4 * gi + 4) * D].rearrange(
            "p (t d) -> p t d", t=4)

    def p3_groupA_head(gi):
        gp = gi % 2
        ts4 = list(range(4 * gi, 4 * gi + 4))
        ok = [f"oacc{t}" for t in ts4]
        for t in ts4:
            pump.add("sync", lambda g, t=t: g.dma_start(
                p3_xr[t % 4][:, :], xrot[t * 128:(t + 1) * 128, :]),
                writes=[f"p3xr{t % 4}"], dma=True)
            pump.add("vector", lambda v, t=t: v.scalar_tensor_tensor(
                out_acc[t], out_acc[t], c_rl[:, t:t + 1], c_bobc[:, :],
                ALU.mult, ALU.add),
                reads=[f"rlt.{t // QT_PER_G}", "bobc"], writes=[f"oacc{t}"])
        kS, kV, kR, kB = (f"gA{gp}.s1", f"gA{gp}.v1", f"gA{gp}.r1", f"gA{gp}.b1")
        pump.add("vector", lambda v, gi=gi, gp=gp: v.reduce_sum(
            st4(gp, 0), oacc3d(gi), axis=AX), reads=ok, writes=[kS])
        for t in ts4:
            pump.add("scalar", lambda s, t=t, gp=gp: s.activation(
                p3_sq[t % 2][:, :], out_acc[t], AF.Square,
                accum_out=st1(gp, 4, t % 4)),
                reads=[f"oacc{t}"], writes=[f"p3sq{t % 2}", kV] if t % 4 == 3 else [f"p3sq{t % 2}", kV])
        pump.add("vector", lambda v, gp=gp: v.scalar_tensor_tensor(
            st4(gp, 12), st4(gp, 0), 1.0 / (D * D), st4(gp, 0), ALU.mult, ALU.mult),
            reads=[kS], writes=[kB])
        pump.add("vector", lambda v, gp=gp: v.scalar_tensor_tensor(
            st4(gp, 4), st4(gp, 4), 1.0 / D, st4(gp, 12), ALU.mult, ALU.subtract),
            reads=[kV, kB], writes=[kV])
        pump.add("scalar", lambda s, gp=gp: s.activation(
            st4(gp, 8), st4(gp, 4), AF.Ln, bias=EPS_LN), reads=[kV], writes=[kR])
        pump.add("scalar", lambda s, gp=gp: s.activation(
            st4(gp, 8), st4(gp, 8), AF.Exp, scale=-0.5), reads=[kR], writes=[kR])
        pump.add("vector", lambda v, gp=gp: v.scalar_tensor_tensor(
            st4(gp, 12), st4(gp, 0), -1.0 / D, st4(gp, 8), ALU.mult, ALU.mult),
            reads=[kS, kR], writes=[kB])
        for t in ts4:
            pump.add("scalar", lambda s, t=t, gp=gp: s.activation(
                p3_ln[t % 2][:, :], out_acc[t], AF.Identity,
                bias=st1(gp, 12, t % 4), scale=st1(gp, 8, t % 4)),
                reads=[f"oacc{t}", kR, kB], writes=[f"p3ln{t % 2}"])
            pump.add("gpsimd", lambda g, t=t: g.tensor_tensor(
                out_acc[t], p3_xr[t % 4][:, :], p3_ln[t % 2][:, :], ALU.add),
                reads=[f"p3xr{t % 4}", f"p3ln{t % 2}"], writes=[f"oacc{t}"])
            pump.add("sync", lambda g, t=t: g.dma_start(
                out[t * 128:(t + 1) * 128, :], out_acc[t]),
                reads=[f"oacc{t}"], writes=[f"outb.{t}"], dma=True)
            if taps:
                pump.add("sync", lambda g, t=t: g.dma_start(
                    x1scr[t * 128:(t + 1) * 128, :], out_acc[t]),
                    reads=[f"oacc{t}"], writes=[f"x1s.{t}"], dma=True)
        kS2, kV2, kR2, kB2 = (f"gA{gp}.s2", f"gA{gp}.v2", f"gA{gp}.r2", f"gA{gp}.b2")
        kLA, kLD = f"gA{gp}.lA", f"gA{gp}.ld"
        pump.add("vector", lambda v, gi=gi, gp=gp: v.reduce_sum(
            st4(gp, 16), oacc3d(gi), axis=AX), reads=ok, writes=[kS2])
        for t in ts4:
            pump.add("scalar", lambda s, t=t, gp=gp: s.activation(
                p3_sq[t % 2][:, :], out_acc[t], AF.Square,
                accum_out=st1(gp, 20, t % 4)),
                reads=[f"oacc{t}"], writes=[f"p3sq{t % 2}", kV2])
        pump.add("vector", lambda v, gp=gp: v.scalar_tensor_tensor(
            st4(gp, 28), st4(gp, 16), 1.0 / (D * D), st4(gp, 16), ALU.mult, ALU.mult),
            reads=[kS2], writes=[kB2])
        pump.add("vector", lambda v, gp=gp: v.scalar_tensor_tensor(
            st4(gp, 20), st4(gp, 20), 1.0 / D, st4(gp, 28), ALU.mult, ALU.subtract),
            reads=[kV2, kB2], writes=[kV2])
        pump.add("scalar", lambda s, gp=gp: s.activation(
            st4(gp, 24), st4(gp, 20), AF.Ln, bias=EPS_LN), reads=[kV2], writes=[kR2])
        pump.add("scalar", lambda s, gp=gp: s.activation(
            st4(gp, 32), st4(gp, 20), AF.Ln, scale=float(D), bias=EPS_L2),
            reads=[kV2], writes=[kLA])
        pump.add("vector", lambda v, gp=gp: v.tensor_sub(
            st4(gp, 36), st4(gp, 32), st4(gp, 24)), reads=[kLA, kR2], writes=[kLD])
        pump.add("scalar", lambda s, gp=gp: s.activation(
            st4(gp, 36), st4(gp, 36), AF.Exp, scale=-0.5), reads=[kLD], writes=[kLD])
        pump.add("scalar", lambda s, gp=gp: s.activation(
            st4(gp, 24), st4(gp, 24), AF.Exp, scale=-0.5), reads=[kR2], writes=[kR2])
        pump.add("vector", lambda v, gp=gp: v.scalar_tensor_tensor(
            st4(gp, 28), st4(gp, 16), -1.0 / D, st4(gp, 24), ALU.mult, ALU.mult),
            reads=[kS2, kR2], writes=[kB2])
    def p3_groupA_tail(gi):
        gp = gi % 2
        ts4 = list(range(4 * gi, 4 * gi + 4))
        kR2, kB2 = f"gA{gp}.r2", f"gA{gp}.b2"
        for t in ts4:
            al = [f"kst1.{2 * (t % 4)}", f"kst1.{2 * (t % 4) + 1}"] if gi == 0 else []
            pump.add("scalar", lambda s, t=t, gp=gp: s.activation(
                p3_ln2[t % 4][:, :], out_acc[t], AF.Identity,
                bias=st1(gp, 28, t % 4), scale=st1(gp, 24, t % 4)),
                reads=[f"oacc{t}", kR2, kB2], writes=[f"p3ln2{t % 4}"] + al)

    def p3_groupA(gi):
        p3_groupA_head(gi)
        p3_groupA_tail(gi)

    if not kv_early[0]:
        kv_load(KC_ORDER[0], 0, first=True)
    if NKC > 1:
        kv_load(KC_ORDER[1], 1, first=True)

    P.fence("scalar", [f"kst{kb}.{ic}" for kb in range(2) for ic in range(DC)]
            + [f"vst.{n}.{j}" for n in range(4) for j in range(2)]
            + (["tapatt"] if taps else []))

    for ki in range(NKC):
        kc = KC_ORDER[ki]
        b = ki % 2
        last = (ki == NKC - 1) and NKC > 1
        if last:
            for _e in ("vector", "scalar", "sync"):
                P.fence(_e, ["kbuf0", "vbuf0"])
        def emit_scores(qg, ks, kc=kc, b=b, last=last):
            ptb = (kc * NQG + qg) % 2
            pb = ks % 2
            qgq_keys = [f"qt.g{qg}.i{ic}" for ic in range(DC)]
            def mms(t, ks=ks, qg=qg, b=b, pb=pb):
                for dc in range(DC):
                    inst = t.matmul(psB[pb][:, 0:QG],
                                    kbuf[b][:, dc, ks * 128:(ks + 1) * 128],
                                    qt_sb[:, dc, qg * QG:(qg + 1) * QG],
                                    start=(dc == 0), stop=(dc == DC - 1))
                return inst
            P.op("tensor", mms, reads=[f"kbuf{b}"] + qgq_keys, writes=[f"psB{pb}"])
            mcol = kc * KS + ks
            P.op("scalar", lambda s, ks=ks, pb=pb, ptb=ptb, mcol=mcol: s.activation(
                pt[ptb][ks][:, :], psB[pb][:, 0:QG], AF.Exp,
                bias=-CSHIFT, scale=c_mask[:, mcol:mcol + 1]),
                reads=[f"psB{pb}", "mask"], writes=[f"pt{ptb}.{ks}"])
            if last:
                pump.pump(2)

        def emit_out(qg, qt, kc=kc, b=b, ki=ki, last=last):
            ptb = (kc * NQG + qg) % 2
            pa = qt % 2
            qglob = qg * QT_PER_G + qt
            def mmo(t, qt=qt, b=b, pa=pa, ptb=ptb):
                for ks in range(KS):
                    st = (ks == 0)
                    sp = (ks == KS - 1)
                    t.matmul(psA[pa][:, 0:512], pt[ptb][ks][:, qt * 128:(qt + 1) * 128],
                             vbuf[b][:, ks, 0:512], start=st, stop=sp)
                    t.matmul(psA[pa][:, 512:1024], pt[ptb][ks][:, qt * 128:(qt + 1) * 128],
                             vbuf[b][:, ks, 512:1024], start=st, stop=sp)
                    inst = t.matmul(psB[2][:, qt:qt + 1],
                                    pt[ptb][ks][:, qt * 128:(qt + 1) * 128],
                                    c_onescol[:, 0:1], start=st, stop=sp)
                return inst
            oalias = (wtq_keys + wtk_keys + wvot_keys) if ki == 0 else []
            P.op("tensor", mmo,
                 reads=[f"pt{ptb}.{ks}" for ks in range(KS)] + [f"vbuf{b}", "onescol"],
                 writes=[f"psA{pa}", "psB2"])
            if ki == 0:
                P.op("vector", lambda v, pa=pa, qglob=qglob: v.tensor_copy(
                    out_acc[qglob], psA[pa][:, 0:D]),
                    reads=[f"psA{pa}"], writes=[f"oacc{qglob}"] + oalias)
            else:
                P.op("vector", lambda v, pa=pa, qglob=qglob: v.tensor_add(
                    out_acc[qglob], out_acc[qglob], psA[pa][:, 0:D]),
                    reads=[f"psA{pa}", f"oacc{qglob}"], writes=[f"oacc{qglob}"])
            if last:
                pump.pump(3)

        for ks in range(KS):
            emit_scores(0, ks)
        for qg in range(NQG):
            kpq = KS // QT_PER_G
            for qt in range(QT_PER_G):
                if qg + 1 < NQG:
                    for kk in range(kpq * qt, kpq * (qt + 1)):
                        emit_scores(qg + 1, kk)
                emit_out(qg, qt)
            if ki == 0:
                P.op("vector", lambda v, qg=qg: v.tensor_copy(
                    c_lacc[:, qg * QT_PER_G:(qg + 1) * QT_PER_G], psB[2][:, 0:QT_PER_G]),
                    reads=["psB2"], writes=[f"lacc.{qg}"])
            else:
                P.op("vector", lambda v, qg=qg: v.tensor_add(
                    c_lacc[:, qg * QT_PER_G:(qg + 1) * QT_PER_G],
                    c_lacc[:, qg * QT_PER_G:(qg + 1) * QT_PER_G], psB[2][:, 0:QT_PER_G]),
                    reads=["psB2", f"lacc.{qg}"],
                    writes=[f"lacc.{qg}"])
            if last:
                sl = slice(qg * QT_PER_G, (qg + 1) * QT_PER_G)
                P.op("scalar", lambda s, sl=sl: s.activation(c_rl[:, sl], c_lacc[:, sl], AF.Ln),
                     reads=[f"lacc.{qg}"], writes=[f"rla.{qg}"])
                P.op("scalar", lambda s, sl=sl: s.activation(c_rl[:, sl], c_rl[:, sl],
                                                             AF.Exp, scale=-1.0),
                     reads=[f"rla.{qg}"], writes=[f"rlt.{qg}"])
                if qg == 0:
                    p3_groupA(0)
        if ki + 2 < NKC:
            kv_load(KC_ORDER[ki + 2], b)

    if NKC == 1:
        for _e in ("vector", "scalar", "sync"):
            P.fence(_e, ["kbuf0", "vbuf0"])
        for qg in range(NQG):
            sl = slice(qg * QT_PER_G, (qg + 1) * QT_PER_G)
            P.op("scalar", lambda s, sl=sl: s.activation(c_rl[:, sl], c_lacc[:, sl], AF.Ln),
                 reads=[f"lacc.{qg}"], writes=[f"rla.{qg}"])
            P.op("scalar", lambda s, sl=sl: s.activation(c_rl[:, sl], c_rl[:, sl],
                                                         AF.Exp, scale=-1.0),
                 reads=[f"rla.{qg}"], writes=[f"rlt.{qg}"])
        p3_groupA(0)
        pump.drain()

    pump.drain()

    dma("gpsimd", wgbsh[:, :, :], swg[:, :].rearrange("(dc p) h -> p dc h", p=128),
        writes=["wgbsh", "kbuf1"] + [f"xg1n{n}" for n in range(4)] + wto_keys)
    dma("gpsimd", wubsh[:, :, :], swu[:, :].rearrange("(dc p) h -> p dc h", p=128),
        writes=["wubsh", "kbuf1"] + [f"xg1n{n}" for n in range(4)] + wto_keys)
    dma("gpsimd", wdbsh[:, :, :], swd[:, :].rearrange("(hc p) d -> p hc d", p=128),
        writes=["wdbsh", "vbuf1", "gwtmp", "gwsq"]
        + [f"ln1t{bb}.n{n}" for bb in range(2) for n in range(4)])

    def p3_groupB(gi):
        gp = gi % 2
        ts4 = list(range(4 * gi, 4 * gi + 4))
        G4 = c_g4[gp]
        for t in ts4:
            i = t % 2
            q = t % 4
            bk, bkey = tbank[i]
            P.op("tensor", lambda t_, q=q, bk=bk: [t_.transpose(
                bk[:, dc * 128:(dc + 1) * 128],
                p3_ln2[q][:, dc * 128:(dc + 1) * 128], c_identb[:, :]) for dc in range(DC)][-1],
                reads=[f"p3ln2{q}", "identb"], writes=[bkey])
            P.op("vector", lambda v, t=t, bk=bk: v.tensor_copy(
                ln2t[:, :, t * 128:(t + 1) * 128],
                bk.rearrange("p (dc j) -> p dc j", dc=DC)),
                reads=[bkey], writes=[f"ln2t.{t}"] + [f"qt.g{t // 4}.i{ic}" for ic in range(DC)])
            def mmg(t_, t=t, i=i):
                for dc in range(DC):
                    inst = t_.matmul(psB[i][:, 0:E_EXP], ln2t[:, dc, t * 128:(t + 1) * 128],
                                     c_gnt[:, dc * E_EXP:(dc + 1) * E_EXP],
                                     start=(dc == 0), stop=(dc == DC - 1))
                return inst
            P.op("tensor", mmg, reads=[f"ln2t.{t}", "gnt"], writes=[f"psB{i}"])
            P.op("vector", lambda v, G4=G4, i=i, q=q, gp=gp: v.tensor_scalar(
                G4[:, q * 64:q * 64 + E_EXP], psB[i][:, 0:E_EXP],
                st1(gp, 36, q), None, ALU.mult),
                reads=[f"psB{i}", f"gA{gp}.ld"], writes=[f"g4log{gp}"])
        def v3(lo):
            return G4[:, :].rearrange("p (t f) -> p t f", t=4)[:, :, lo:lo + E_EXP]
        kSg, kWk, kM = f"g4sig{gp}", f"g4wk{gp}", f"g4m{gp}"
        thr = st4(gp, 40)
        thrb = bc16(gp, 40)
        pump.add("scalar", lambda s, G4=G4: s.activation(
            v3(16), v3(0), AF.Sigmoid), reads=[f"g4log{gp}"], writes=[kSg])
        pump.add("vector", lambda v, G4=G4: v.tensor_copy(v3(32), v3(16)),
                 reads=[kSg], writes=[kWk])
        for it in range(TOPK):
            pump.add("vector", lambda v, G4=G4, thr=thr: v.reduce_max(
                thr, v3(32), axis=AX), reads=[kWk], writes=[kM])
            if it < TOPK - 1:
                pump.add("vector", lambda v, G4=G4, thrb=thrb: v.tensor_tensor(
                    v3(48), v3(32), thrb, ALU.is_ge), reads=[kWk, kM], writes=[f"g4t{gp}"])
                pump.add("vector", lambda v, G4=G4: v.scalar_tensor_tensor(
                    v3(32), v3(48), -BIG, v3(32), ALU.mult, ALU.add),
                    reads=[f"g4t{gp}", kWk], writes=[kWk])
        pump.add("vector", lambda v, G4=G4, thrb=thrb: v.tensor_tensor(
            v3(32), v3(16), thrb, ALU.is_ge), reads=[kSg, kM], writes=[kWk])
        pump.add("vector", lambda v, G4=G4: v.tensor_tensor(
            v3(32), v3(32), v3(16), ALU.mult), reads=[kWk, kSg], writes=[kWk])
        pump.add("vector", lambda v, G4=G4, thr=thr: v.reduce_sum(
            thr, v3(32), axis=AX), reads=[kWk], writes=[kM])
        pump.add("scalar", lambda s, thr=thr: s.activation(thr, thr, AF.Ln),
                 reads=[kM], writes=[kM])
        pump.add("scalar", lambda s, thr=thr: s.activation(thr, thr, AF.Exp, scale=-1.0),
                 reads=[kM], writes=[kM])
        ckeys = [f"comb.{t}" for t in ts4]
        pump.add("vector", lambda v, G4=G4, gi=gi, thrb=thrb: v.tensor_tensor(
            c_comb[:, 4 * gi * E_EXP:(4 * gi + 4) * E_EXP].rearrange(
                "p (t f) -> p t f", t=4),
            v3(32), thrb, ALU.mult),
            reads=[kWk, kM], writes=ckeys)

    def ln2t_gkeys(j):
        return [f"ln2t.{tt}" for tt in range(4 * j, 4 * j + 4)]

    act_first = set()
    ew_first = [True]
    lt1k = [f"ln1t1.n{n}" for n in range(4)]
    slot_alias_map = {0: None, 1: None, 2: None, 3: None,
                      4: ["vbuf1"] + lt1k, 5: ["vbuf1"] + lt1k}

    def gu_pair(wgG, wuG, wgkey, wukey, hc, j, it2, aslot, has_cb, cbx, aalias=()):
        aalias = list(aalias)
        if aslot not in act_first:
            act_first.add(aslot)
            sa = slot_alias_map[aslot]
            if sa is None:
                sa = [f"pt{bb}.{k}" for bb in range(2) for k in range(KS)]
            aalias += sa
        pg = (it2 * 2) % 3
        pu = (it2 * 2 + 1) % 3
        P.op("tensor", lambda t_, hc=hc, j=j, pg=pg, wgG=wgG: [t_.matmul(
            psB[pg][:, :], wgG[:, dc, hc * 128:(hc + 1) * 128],
            ln2t[:, dc, j * 512:(j + 1) * 512],
            start=(dc == 0), stop=(dc == DC - 1)) for dc in range(DC)][-1],
            reads=[wgkey] + ln2t_gkeys(j), writes=[f"psB{pg}"])
        ewal = ["p1sq1"] if ew_first[0] else []
        ew_first[0] = False
        P.op("scalar", lambda s, pg=pg: s.activation(
            ew[0][:, :], psB[pg][:, :], AF.Silu),
            reads=[f"psB{pg}"], writes=["ew0"] + ewal)
        P.op("tensor", lambda t_, hc=hc, j=j, pu=pu, wuG=wuG: [t_.matmul(
            psB[pu][:, :], wuG[:, dc, hc * 128:(hc + 1) * 128],
            ln2t[:, dc, j * 512:(j + 1) * 512],
            start=(dc == 0), stop=(dc == DC - 1)) for dc in range(DC)][-1],
            reads=[wukey] + ln2t_gkeys(j), writes=[f"psB{pu}"])
        P.op("vector", lambda v, pu=pu, aslot=aslot, j=j: v.tensor_tensor(
            actt[aslot][:, j * 512:(j + 1) * 512], ew[0][:, :], psB[pu][:, :], ALU.mult),
            reads=["ew0", f"psB{pu}"], writes=[f"act.{aslot}.{j}"] + list(aalias) + ewal)

    def p5_tile(t):
        i = t % 2
        al_x = ["wgb0"] if t == 0 else []
        r, bb, _, kR, kB, _ = ln_stats_ops(P, routed[t][:, :], p1_sq[0][:, :],
                                           "p1sq0", 4 + 2 * i, f"routed{t}")
        P.op("scalar", lambda s, t=t, i=i, r=r, bb=bb: s.activation(
            p3_af[i][:, :], routed[t][:, :], AF.Identity, bias=bb, scale=r),
            reads=[f"routed{t}", kR, kB], writes=[f"p3af{i}"] + al_x)
        P.op("gpsimd", lambda g, t=t, i=i: g.dma_start(
            out[t * 128:(t + 1) * 128, :], p3_af[i][:, :], accum_op=ALU.add),
            reads=[f"p3af{i}", f"outb.{t}"], writes=[f"outw.{t}"], dma=True)

    def down_unit(u_first, aslots, wdG, wdkey, e_comb=None, tail=None, tiles=None):
        for t in (range(NT) if tiles is None else tiles):
            act_keys = [f"act.{a}.{t // 4}" for a in aslots]
            pa = t % 2
            def mmd(t_, t=t, pa=pa, aslots=aslots, wdG=wdG):
                for hi, a in enumerate(aslots):
                    st = (hi == 0)
                    sp = (hi == HC - 1)
                    t_.matmul(psA[pa][:, 0:512], actt[a][:, t * 128:(t + 1) * 128],
                              wdG[:, hi, 0:512], start=st, stop=sp)
                    inst = t_.matmul(psA[pa][:, 512:1024], actt[a][:, t * 128:(t + 1) * 128],
                                     wdG[:, hi, 512:1024], start=st, stop=sp)
                return inst
            P.op("tensor", mmd, reads=act_keys + [wdkey], writes=[f"psA{pa}"])
            if e_comb is None:
                if u_first:
                    P.op("vector", lambda v, pa=pa, t=t: v.tensor_copy(
                        routed[t][:, :], psA[pa][:, 0:D]),
                        reads=[f"psA{pa}"], writes=[f"routed{t}", f"oacc{t}"])
                else:
                    P.op("vector", lambda v, pa=pa, t=t: v.tensor_add(
                        routed[t][:, :], routed[t][:, :], psA[pa][:, 0:D]),
                        reads=[f"psA{pa}", f"routed{t}"], writes=[f"routed{t}"])
            else:
                P.op("vector", lambda v, pa=pa, t=t, e=e_comb: v.scalar_tensor_tensor(
                    routed[t][:, :], psA[pa][:, 0:D],
                    c_comb[:, t * E_EXP + e:t * E_EXP + e + 1],
                    routed[t][:, :], ALU.mult, ALU.add),
                    reads=[f"psA{pa}", f"routed{t}", f"comb.{t}"], writes=[f"routed{t}"])
            if tail is not None:
                tail(t)

    SH_SLOTS = [0, 1, 2, 3]
    for j in range(NGT):
        if j + 1 < NGT:
            p3_groupA_head(j + 1)
        p3_groupB(j)
        if j + 1 < NGT:
            p3_groupA_tail(j + 1)
        for hc in range(HC):
            gu_pair(wgbsh, wubsh, "wgbsh", "wubsh", hc, j, j * HC + hc,
                    SH_SLOTS[hc], False, 0)
            pump.pump(12)
        down_unit(True, SH_SLOTS, wdbsh, "wdbsh",
                  tiles=range(4 * j, min(4 * j + 4, NT)))
        pump.drain()

    if taps:
        P.op("sync", lambda g: g.dma_start(taps_t["t_x1"][:, :], x1scr[:, :]),
             reads=[f"x1s.{t}" for t in range(NT)], writes=["tapx1"], dma=True)
        P.op("sync", lambda g: g.dma_start(taps_t["t_cb"][:, :], c_comb[:, :]),
             reads=[f"comb.{t}" for t in range(NT)], writes=["tapcb"], dma=True)

    def wload(e, b):
        al_g = (["p3xr2", "p3xr3", "p3af0", "p3af1", "kbuf0"]) if e < 2 else []
        al_u = ([f"p3sq{i}" for i in range(2)] + ["wgbsh", "wubsh", "kbuf1"]) if e < 2 else []
        dma("gpsimd", wgb[b][:, :, :],
            wgate[e:e + 1, :, :].rearrange("e (dc p) h -> p (e dc) h", p=128),
            writes=[f"wgb{b}"] + al_g)
        dma("gpsimd", wub[b][:, :, :],
            wup[e:e + 1, :, :].rearrange("e (dc p) h -> p (e dc) h", p=128),
            writes=[f"wub{b}"] + (["wubsh"] if e < 2 else []) + al_u)

    def wdload(e):
        wb = e % 2
        if e == 0:
            al = [f"p3ln{i}" for i in range(2)]
        elif e == 1:
            al = (["wdbsh", "vbuf1", "gwtmp", "gwsq"]
                  + [f"ln1t{bb}.n{n}" for bb in range(NLT) for n in range(4)])
        else:
            al = []
        dma("gpsimd", wdb[wb][:, :, :],
            wdown[e:e + 1, :, :].rearrange("e (hc p) d -> p (e hc) d", p=128),
            writes=[f"wdb{wb}"] + al)

    wload(0, 0)
    wdload(0)
    for e in range(E_EXP):
        b = e % 2
        if e + 1 < E_EXP:
            wload(e + 1, (e + 1) % 2)
            wdload(e + 1)
        aslots = [(4 + e * HC + hc) % 6 for hc in range(HC)]
        if e < E_EXP - 1:
            for hc in range(HC):
                for j in range(NGT):
                    gu_pair(wgb[b], wub[b], f"wgb{b}", f"wub{b}", hc, j,
                            hc * NGT + j, aslots[hc], True, 0)
            if e < E_EXP - 2:
                down_unit(False, aslots, wdb[b], f"wdb{b}", e_comb=e)
        else:
            asl_prev = [(4 + (e - 1) * HC + hc) % 6 for hc in range(HC)]
            for j in range(NGT):
                down_unit(False, asl_prev, wdb[1 - b], f"wdb{1 - b}",
                          e_comb=e - 1, tiles=range(4 * j, 4 * j + 4))
                for hc in range(HC):
                    gu_pair(wgb[b], wub[b], f"wgb{b}", f"wub{b}", hc, j,
                            j * HC + hc, aslots[hc], True, 0)
                down_unit(False, aslots, wdb[b], f"wdb{b}", e_comb=e,
                          tail=p5_tile, tiles=range(4 * j, 4 * j + 4))

    if taps:
        for t in range(NT):
            dma("sync", taps_t["t_ffn"][t * 128:(t + 1) * 128, :], routed[t][:, :],
                reads=[f"routed{t}"], writes=[f"tapffn{t}"])

    for q in Prog.DMAQ:
        for i in range(Prog.NSLOT):
            ch = f"{q}.d{i}"
            if P.cnt[ch]:
                P.ops[q].append(lambda g, s=P.sem[ch], c=P.cnt[ch]: g.wait_ge(s, c))

    P.render()
    return nc, taps_t


def _mask_pp(s_batch, q0):
    pos = (q0 + np.arange(s_batch)) % S
    m = np.exp(-0.5 * ((pos - S / 2.0) / (S / 4.0)) ** 2) / np.sqrt(D)
    return np.ascontiguousarray(m.reshape(s_batch // 128, 128).T.astype(np.float32))


def make_inputs(x, Wq, bq, Wk, bk, Wv, bv, Wo, bo, gate_w, w_gate, w_up, w_down,
                sw_gate, sw_up, sw_down, nt_core, s_batch, cores, ag=True):
    bqa = np.asarray(bq, np.float32).reshape(DC, 128).T
    bka = np.asarray(bk, np.float32).reshape(DC, 128).T
    bva = np.asarray(bv, np.float32).reshape(DC, 128).T
    boa = np.asarray(bo, np.float32).reshape(DC, 128).T
    bqkvo = np.ascontiguousarray(np.concatenate([bqa, bka, bva, boa], axis=1).astype(np.float32))
    constsin = np.broadcast_to(np.array(CONSTS, np.float32), (128, len(CONSTS))).copy()
    common = dict(
        wq=np.ascontiguousarray(Wq, np.float32), wk=np.ascontiguousarray(Wk, np.float32),
        wv=np.ascontiguousarray(Wv, np.float32), wo=np.ascontiguousarray(Wo, np.float32),
        bqkvo=bqkvo,
        borow=np.ascontiguousarray(np.asarray(bo, np.float32).reshape(1, D)),
        gatew=np.ascontiguousarray(gate_w, np.float32),
        wgate=np.ascontiguousarray(w_gate, np.float32),
        wup=np.ascontiguousarray(w_up, np.float32),
        wdown=np.ascontiguousarray(w_down, np.float32),
        swg=np.ascontiguousarray(sw_gate, np.float32),
        swu=np.ascontiguousarray(sw_up, np.float32),
        swd=np.ascontiguousarray(sw_down, np.float32),
        identf=np.eye(128, dtype=np.float32),
        identb=np.eye(128, dtype=np.float32).astype(ml_dtypes.bfloat16),
        onesrow=np.ones((1, 128), np.float32),
        onescol=np.ones((128, 1), np.float32).astype(ml_dtypes.bfloat16),
        constsin=constsin,
    )
    in_maps = []
    for core in range(cores):
        bidx = core // (cores // B)
        q0 = (core % (cores // B)) * nt_core
        im = dict(common)
        if ag:
            im["xrot"] = np.ascontiguousarray(x[bidx][q0:q0 + nt_core])
            im["maskpp"] = _mask_pp(s_batch, 0)
        else:
            im["xrot"] = np.ascontiguousarray(np.roll(x[bidx][:s_batch], -q0, axis=0))
            im["maskpp"] = _mask_pp(s_batch, q0)
        in_maps.append(im)
    return in_maps


def kernel(L0_attn_x, L0_attn_Wq, L0_attn_bq, L0_attn_Wk, L0_attn_bk,
           L0_attn_Wv, L0_attn_bv, L0_attn_Wo, L0_attn_bo,
           L0_ffn_gate_w, L0_ffn_w_gate, L0_ffn_w_up, L0_ffn_w_down,
           L0_ffn_sw_gate, L0_ffn_sw_up, L0_ffn_sw_down):
    x = np.ascontiguousarray(np.asarray(L0_attn_x, dtype=np.float32))
    nt_core = B * S // NCORES
    nc, _ = build(nt_core=nt_core, s_batch=S, taps=False, ag=True)
    in_maps = make_inputs(x, L0_attn_Wq, L0_attn_bq, L0_attn_Wk, L0_attn_bk,
                          L0_attn_Wv, L0_attn_bv, L0_attn_Wo, L0_attn_bo,
                          L0_ffn_gate_w, L0_ffn_w_gate, L0_ffn_w_up, L0_ffn_w_down,
                          L0_ffn_sw_gate, L0_ffn_sw_up, L0_ffn_sw_down,
                          nt_core, S, NCORES)
    res = run_bass_kernel_spmd(nc, in_maps, core_ids=list(range(NCORES)))
    outa = np.empty((B, S, D), np.float32)
    for core in range(NCORES):
        bidx = core // (NCORES // B)
        q0 = (core % (NCORES // B)) * nt_core
        outa[bidx, q0:q0 + nt_core, :] = np.asarray(res.results[core]["out"])
    return outa


# version 28 (final)
# speedup vs baseline: 1.0021x; 1.0021x over previous
import numpy as np
import ml_dtypes
import concourse.bass as bass
import concourse.mybir as mybir
from concourse.bass_utils import run_bass_kernel_spmd

F32 = mybir.dt.float32
BF16 = mybir.dt.bfloat16
AF = mybir.ActivationFunctionType
ALU = mybir.AluOpType
AX = mybir.AxisListType.X

B, S, D = 2, 8192, 1024
NCORES = 8
E_EXP, HID = 16, 512
TOPK = 8
DC = D // 128
HC = HID // 128
CSHIFT = 12.0
EPS_LN = 1e-5
EPS_L2 = 1e-12
BIG = 100.0

CONSTS = [-CSHIFT, 1.0 / D, -1.0, -0.5, 0.5, EPS_LN, EPS_L2, BIG, float(D), 1.0, 0.0, 1.0 / (D * D), -1.0 / D]


class Prog:
    ENGS = ("tensor", "vector", "scalar", "gpsimd", "sync")
    DMAQ = ("sync", "gpsimd")
    NSLOT = 8

    def __init__(self, nc):
        self.nc = nc
        self.chans = list(self.ENGS) + [f"{q}.d{i}" for q in self.DMAQ
                                        for i in range(self.NSLOT)]
        self.sem = {c: nc.alloc_semaphore(f"qs_{c}") for c in self.chans}
        self.cnt = {c: 0 for c in self.chans}
        self.clock = {e: {c: 0 for c in self.chans} for e in self.ENGS}
        self.opclock = {}
        self.w = {}
        self.r = {}
        self.ops = {e: [] for e in self.ENGS}
        self.dmarr = {q: 0 for q in self.DMAQ}
        self.nwaits = 0

    def _need(self, eng, dep):
        d_ch, d_cnt = dep
        if d_ch == eng == "tensor":
            return
        if self.clock[eng][d_ch] >= d_cnt:
            return
        sem = self.sem[d_ch]
        self.ops[eng].append(lambda g, s=sem, c=d_cnt: g.wait_ge(s, c))
        self.nwaits += 1
        pc = self.opclock.get((d_ch, d_cnt))
        if pc:
            ce = self.clock[eng]
            for k, v in pc.items():
                if v > ce[k]:
                    ce[k] = v
        if d_cnt > self.clock[eng][d_ch]:
            self.clock[eng][d_ch] = d_cnt

    def op(self, eng, fn, reads=(), writes=(), dma=False):
        for k in reads:
            if k in self.w:
                self._need(eng, self.w[k])
        for k in writes:
            if k in self.w:
                self._need(eng, self.w[k])
            for dep in self.r.get(k, ()):
                self._need(eng, dep)
        if dma:
            slot = self.dmarr[eng]
            self.dmarr[eng] = (slot + 1) % self.NSLOT
            ch = f"{eng}.d{slot}"
            self._need(eng, (ch, self.cnt[ch]))
            self.cnt[ch] += 16
            cnt = self.cnt[ch]
            sem = self.sem[ch]

            def run(g, fn=fn, sem=sem):
                fn(g).then_inc(sem, 16)
            self.ops[eng].append(run)
            oc = dict(self.clock[eng])
            oc[ch] = cnt
            self.opclock[(ch, cnt)] = oc
            tok = (ch, cnt)
        else:
            self.cnt[eng] += 1
            cnt = self.cnt[eng]
            sem = self.sem[eng]

            def run(g, fn=fn, sem=sem):
                fn(g).then_inc(sem, 1)
            self.ops[eng].append(run)
            oc = dict(self.clock[eng])
            oc[eng] = cnt
            self.opclock[(eng, cnt)] = oc
            tok = (eng, cnt)
        for k in writes:
            self.w[k] = tok
            self.r[k] = []
        for k in reads:
            self.r.setdefault(k, []).append(tok)
        return tok

    def fence(self, eng, keys):
        for k in keys:
            if k in self.w:
                self._need(eng, self.w[k])
            for dep in self.r.get(k, ()):
                self._need(eng, dep)

    def render(self):
        with self.nc.Block() as blk:
            for e in self.ENGS:
                ops = self.ops[e]
                if ops:
                    getattr(blk, e)(lambda g, ops=ops: [f(g) for f in ops])


class Pump:

    def __init__(self, P):
        self.P = P
        self.q = []

    def add(self, eng, fn, reads=(), writes=(), dma=False):
        self.q.append((eng, fn, reads, writes, dma))

    def pump(self, n):
        for _ in range(min(n, len(self.q))):
            eng, fn, reads, writes, dma = self.q.pop(0)
            self.P.op(eng, fn, reads=reads, writes=writes, dma=dma)

    def drain(self):
        self.pump(len(self.q))


def build(nt_core=2048, s_batch=S, taps=False, ag=True, ag_groups=None):
    assert nt_core % 512 == 0 and s_batch % 512 == 0
    NT = nt_core // 128
    NRANK = s_batch // nt_core
    if ag_groups is None:
        ag_groups = [[0, 1, 2, 3], [4, 5, 6, 7]] if (NRANK == 4 and NCORES == 8) \
            else [list(range(NRANK))]
    NG = (nt_core // 512) if ag else (s_batch // 512)
    NGQ = nt_core // 512
    KCHUNK = min(1024, nt_core if ag else s_batch)
    NKC = s_batch // KCHUNK
    CPR = nt_core // KCHUNK if ag else 0
    SPLIT_AG = ag and CPR == 2
    if SPLIT_AG:
        KC_ORDER = [k for k in range(NKC) if k % 2 == 0] + \
                   [k for k in range(NKC) if k % 2 == 1]
    else:
        KC_ORDER = list(range(NKC))
    KS = KCHUNK // 128
    QG = 512
    NQG = nt_core // QG
    QT_PER_G = QG // 128
    MCOL = s_batch // 128
    NGT = nt_core // 512

    nc = bass.Bass("TRN2", target_bir_lowering=False)

    xrot = nc.dram_tensor("xrot", [nt_core if ag else s_batch, D], F32, kind="ExternalInput")
    wq = nc.dram_tensor("wq", [D, D], F32, kind="ExternalInput")
    wk = nc.dram_tensor("wk", [D, D], F32, kind="ExternalInput")
    wv = nc.dram_tensor("wv", [D, D], F32, kind="ExternalInput")
    wo = nc.dram_tensor("wo", [D, D], F32, kind="ExternalInput")
    bqkvo = nc.dram_tensor("bqkvo", [128, 4 * DC], F32, kind="ExternalInput")
    borow = nc.dram_tensor("borow", [1, D], F32, kind="ExternalInput")
    maskpp = nc.dram_tensor("maskpp", [128, MCOL], F32, kind="ExternalInput")
    gatew = nc.dram_tensor("gatew", [E_EXP, D], F32, kind="ExternalInput")
    wgate = nc.dram_tensor("wgate", [E_EXP, D, HID], F32, kind="ExternalInput")
    wup = nc.dram_tensor("wup", [E_EXP, D, HID], F32, kind="ExternalInput")
    wdown = nc.dram_tensor("wdown", [E_EXP, HID, D], F32, kind="ExternalInput")
    swg = nc.dram_tensor("swg", [D, HID], F32, kind="ExternalInput")
    swu = nc.dram_tensor("swu", [D, HID], F32, kind="ExternalInput")
    swd = nc.dram_tensor("swd", [HID, D], F32, kind="ExternalInput")
    identf = nc.dram_tensor("identf", [128, 128], F32, kind="ExternalInput")
    identb = nc.dram_tensor("identb", [128, 128], BF16, kind="ExternalInput")
    onesrow = nc.dram_tensor("onesrow", [1, 128], F32, kind="ExternalInput")
    onescol = nc.dram_tensor("onescol", [128, 1], BF16, kind="ExternalInput")
    constsin = nc.dram_tensor("constsin", [128, len(CONSTS)], F32, kind="ExternalInput")
    out = nc.dram_tensor("out", [nt_core, D], F32, kind="ExternalOutput")

    kscr = nc.dram_tensor("kscr", [D, s_batch], BF16, kind="Internal")
    vscr = nc.dram_tensor("vscr", [s_batch, D], BF16, kind="Internal")
    kag_in = nc.dram_tensor("kag_in", [D, nt_core], BF16, kind="Internal")
    kag_inh = [nc.dram_tensor(f"kag_inh{h}", [D, nt_core // 2], BF16, kind="Internal")
               for h in range(2)]
    vag_in = nc.dram_tensor("vag_in", [nt_core, D], BF16, kind="Internal")
    kag_out = [nc.dram_tensor(f"kag_out{h}", [NRANK * D, nt_core // 2], BF16, kind="Internal")
               for h in range(2)]
    vag_out = [nc.dram_tensor(f"vag_out{h}", [s_batch // 2, D], BF16, kind="Internal")
               for h in range(2)]
    kag_outF = nc.dram_tensor("kag_outF", [NRANK * D, nt_core], BF16, kind="Internal")
    vag_outF = nc.dram_tensor("vag_outF", [s_batch, D], BF16, kind="Internal")
    x1scr = nc.dram_tensor("x1scr", [nt_core, D], F32, kind="Internal")

    taps_t = {}
    if taps:
        taps_t["t_x1"] = nc.dram_tensor("t_x1", [nt_core, D], F32, kind="ExternalOutput")
        taps_t["t_cb"] = nc.dram_tensor("t_cb", [128, (nt_core // 128) * E_EXP], F32, kind="ExternalOutput")
        taps_t["t_ffn"] = nc.dram_tensor("t_ffn", [nt_core, D], F32, kind="ExternalOutput")
        taps_t["t_att"] = nc.dram_tensor("t_att", [D, s_batch], BF16, kind="ExternalOutput")

    SB_BASE = 16512
    ptr = [SB_BASE]

    def sba(name, shape, dtype, at=None):
        per = int(np.prod(shape[1:])) * mybir.dt.size(dtype)
        per = (per + 31) // 32 * 32
        if at is None:
            at = ptr[0]
            ptr[0] = at + per
        else:
            ptr[0] = max(ptr[0], at + per)
        return nc.alloc_sbuf_tensor_at(name, list(shape), dtype, offset=at)

    c_ident = sba("c_ident", [128, 128], F32)
    c_identb = sba("c_identb", [128, 128], BF16)
    c_ones = sba("c_ones", [1, 128], F32)
    c_onescol = sba("c_onescol", [128, 1], BF16)
    c_mask = sba("c_mask", [128, MCOL], F32)
    c_bias = sba("c_bias", [128, 4 * DC], F32)
    c_const = sba("c_const", [128, len(CONSTS)], F32)
    c_borow = sba("c_borow", [1, D], F32)
    c_bobc = sba("c_bobc", [128, D], F32)
    c_gnt = sba("c_gnt", [128, DC * E_EXP], BF16)
    c_gwbf = sba("c_gwbf", [E_EXP, D], BF16)
    c_lacc = sba("c_lacc", [128, NT], F32)
    c_rl = sba("c_rl", [128, NT], F32)
    c_st = sba("c_st", [128, 160], F32)
    c_comb = sba("c_comb", [128, (nt_core // 128) * E_EXP], F32)
    c_g4 = [sba(f"c_g4_{i}", [128, 256], F32) for i in range(2)]
    c_bvb = sba("c_bvb", [128, DC], BF16)
    for i, v in enumerate(CONSTS):
        nc.const_aps.register(F32, float(v), c_const[:, i:i + 1])

    offA = ptr[0]
    qt_sb = sba("qt_sb", [128, DC, nt_core], BF16, at=offA)
    ln2t = sba("ln2t", [128, DC, nt_core], BF16, at=offA)

    offB = ptr[0]
    WTQ = sba("wtq", [128, DC, D], BF16, at=offB)
    WTK = sba("wtk", [128, DC, D], BF16, at=offB + DC * D * 2)
    WVOT = sba("wvot", [128, DC, D], BF16, at=offB + 2 * DC * D * 2)
    oaccA = sba("oaccA", [128, NT * D], F32, at=offB)
    out_acc = [oaccA[:, t * D:(t + 1) * D] for t in range(NT)]
    routed = [sba(f"routed{t}", [128, D], F32, at=offB + t * D * 4) for t in range(NT)]
    ptr[0] = offB + max(NT * D * 4, 3 * DC * D * 2)

    offC = ptr[0]
    wn_tmp = sba("wn_tmp", [128, DC, D], BF16, at=offC)
    wto = sba("wto", [128, DC, D], BF16, at=offC + 16384)
    wvn = sba("wvn", [128, DC, D], BF16, at=offC + 32768)
    gw_tmp = sba("gw_tmp", [E_EXP, D], F32, at=offC + 49152)
    gwsq = sba("gwsq", [E_EXP, D], F32, at=offC + 53248)
    if ag:
        xg = [sba("xg0", [128, 4 * D], F32, at=offC)]
        ln1g = [sba(f"ln1g{b}", [128, 4 * D], BF16, at=offC + 16384 + b * 8192) for b in range(2)]
        ln1t = [sba(f"ln1t{b}", [128, DC, 512], BF16, at=offC + 32768 + b * 8192)
                for b in range(4)]
    else:
        xg = [sba(f"xg{b}", [128, 4 * D], F32, at=offC + b * 16384) for b in range(2)]
        ln1g = [sba(f"ln1g{b}", [128, 4 * D], BF16, at=offC + 32768 + b * 8192) for b in range(2)]
        ln1t = [sba(f"ln1t{b}", [128, DC, 512], BF16, at=offC + 49152 + b * 8192) for b in range(2)]
    NXG = len(xg)
    NLT = len(ln1t)
    kbuf = [sba(f"kbuf{b}", [128, DC, KCHUNK], BF16, at=offC + b * 16384) for b in range(2)]
    vbuf = [sba(f"vbuf{b}", [128, KS, D], BF16, at=offC + 32768 + b * 16384) for b in range(2)]
    p3_af = [sba(f"p3_af{i}", [128, D], F32, at=offC + i * 4096) for i in range(2)]
    p3_sq = [sba(f"p3_sq{i}", [128, D], F32, at=offC + 8192 + i * 4096) for i in range(2)]
    p3_ln = [sba(f"p3_ln{i}", [128, D], F32, at=offC + 32768 + i * 4096) for i in range(2)]
    p3_xr = [sba(f"p3_xr{i}", [128, D], F32, at=offC + 40960 + i * 4096) for i in range(2)] \
        + [sba(f"p3_xr{i + 2}", [128, D], F32, at=offC + i * 4096) for i in range(2)]
    wgbsh = sba("wgbsh", [128, DC, HID], BF16, at=offC + 16384)
    wubsh = sba("wubsh", [128, DC, HID], BF16, at=offC + 24576)
    wdbsh = sba("wdbsh", [128, HC, D], BF16, at=offC + 49152)
    wgb = [sba(f"wgb{b}", [128, DC, HID], BF16, at=offC + b * 8192) for b in range(2)]
    wub = [sba(f"wub{b}", [128, DC, HID], BF16, at=offC + 16384 + b * 8192) for b in range(2)]
    wdb = [sba("wdb", [128, HC, D], BF16, at=offC + 32768),
           sba("wdb1", [128, HC, D], BF16, at=offC + 49152)]
    ptr[0] = offC + 65536

    offD = ptr[0]
    kst = [sba("kst", [128, DC, 512], BF16, at=offD),
           sba("kst1", [128, DC, 512], BF16, at=offD + 16384)]
    vst = [sba("vst", [128, 4, D], BF16, at=offD + 8192)]
    vst.append(sba("vst1", [128, 4, D], BF16, at=offA) if ag else vst[0])
    pt = [[sba(f"pt{b}_{k}", [128, QG], BF16, at=offD + (b * KS + k) * QG * 2)
           for k in range(KS)] for b in range(2)]
    p3_ln2 = [sba(f"p3_ln2_{i}", [128, D], BF16, at=offD + 16384 + i * 2048) for i in range(4)]
    actt = [sba(f"actt{k}", [128, nt_core], BF16, at=offD + k * 4096) for k in range(4)]
    actt += [sba(f"actt{k + 4}", [128, nt_core], BF16, at=offC + 57344 + k * 4096)
             for k in range(2)]
    ew = [sba("ew0", [128, 512], F32, at=offD + 24576),
          sba("ew1", [128, 512], F32, at=offD + 26624)]
    p1_sq = [sba("p1_sq0", [128, D], F32, at=offD + 28672),
             sba("p1_sq1", [128, D], F32, at=offD + 24576)]
    ptr[0] = offD + 32768

    assert ptr[0] <= 229300, f"SBUF over budget: {ptr[0]}"

    psA = [nc.alloc_psum_tensor(f"psA{i}", [128, 1024], F32) for i in range(2)]
    psB = [nc.alloc_psum_tensor(f"psB{i}", [128, 512], F32) for i in range(3)]
    ps7b = nc.alloc_psum_tensor("ps7b", [128, 1024], BF16)
    psB2b = psB[2][:, :].bitcast(BF16)
    tbank = [(ps7b[:, :], "ps7b"), (psB2b, "psB2")]

    P = Prog(nc)

    def dma(q, dst, src, reads=(), writes=()):
        return P.op(q, lambda g: g.dma_start(dst, src), reads=reads, writes=writes, dma=True)

    dma("sync", c_ident[:, :], identf[:, :], writes=["ident"])
    dma("sync", c_identb[:, :], identb[:, :], writes=["identb"])
    dma("sync", c_ones[:, :], onesrow[:, :], writes=["ones"])
    dma("sync", c_onescol[:, :], onescol[:, :], writes=["onescol"])
    dma("sync", c_mask[:, :], maskpp[:, :], writes=["mask"])
    dma("sync", c_bias[:, :], bqkvo[:, :], writes=["bias"])
    dma("sync", c_const[:, :], constsin[:, :], writes=["const"])
    dma("sync", c_borow[:, :], borow[:, :], writes=["borow"])
    dma("sync", gw_tmp[:, :], gatew[:, :], writes=["gwtmp"])
    for _e in Prog.ENGS:
        P.fence(_e, ["const", "ident", "identb", "ones", "onescol", "mask", "bias"])

    HR = DC // 2

    QR = DC // 4

    def cast_wt(wdram, stage, tag, prev):
        for h in range(4):
            extra = [f"wn_{prev}h{h}"] if prev else []
            dma("gpsimd", stage[:, h * QR:(h + 1) * QR, :],
                wdram[h * QR * 128:(h + 1) * QR * 128, :].rearrange(
                    "(r p) d -> p r d", p=128),
                writes=[f"wn_{tag}h{h}"] + extra)

    def trans_wt(stage, wt_sb, tag):
        for r in range(DC):
            bk, bkey = tbank[r % 2]
            h = r // QR
            P.op("tensor",
                 lambda t, r=r, bk=bk, stage=stage: [t.transpose(
                     bk[:, dc * 128:(dc + 1) * 128],
                     stage[:, r, dc * 128:(dc + 1) * 128],
                     c_identb[:, :]) for dc in range(DC)][-1],
                 reads=[f"wn_{tag}h{h}", "identb"], writes=[bkey])
            P.op("vector",
                 lambda v, r=r, wt_sb=wt_sb, bk=bk: v.tensor_copy(
                     wt_sb[:, :, r * 128:(r + 1) * 128],
                     bk.rearrange("p (dc j) -> p dc j", dc=DC)),
                 reads=[bkey], writes=[f"wt_{tag}.r{r}"])

    cast_wt(wq, wn_tmp, "q", None)
    cast_wt(wv, wvn, "v", None)
    trans_wt(wn_tmp, WTQ, "q")
    cast_wt(wk, wn_tmp, "k", "q")
    trans_wt(wn_tmp, WTK, "k")
    cast_wt(wo, wn_tmp, "o", "k")
    trans_wt(wn_tmp, wto, "o")
    wvn_keys = [f"wn_vh{h}" for h in range(4)]

    wto_keys = [f"wt_o.r{r}" for r in range(DC)]
    wtq_keys = [f"wt_q.r{r}" for r in range(DC)]
    wtk_keys = [f"wt_k.r{r}" for r in range(DC)]
    for db in range(DC):
        for j in range(2):
            pb = (db * 2 + j) % 2
            def mmwvo(t, db=db, j=j, pb=pb):
                for cc in range(DC):
                    inst = t.matmul(psB[pb][:, :],
                                    wvn[:, cc, db * 128:(db + 1) * 128],
                                    wto[:, cc, j * 512:(j + 1) * 512],
                                    start=(cc == 0), stop=(cc == DC - 1))
                return inst
            P.op("tensor", mmwvo, reads=wvn_keys + wto_keys, writes=[f"psB{pb}"])
            P.op("vector", lambda v, db=db, j=j, pb=pb: v.tensor_copy(
                WVOT[:, db, j * 512:(j + 1) * 512], psB[pb][:, :]),
                reads=[f"psB{pb}"], writes=[f"wvot.{db}.{j}"])
    wvot_keys = [f"wvot.{db}.{j}" for db in range(DC) for j in range(2)]

    P.op("scalar", lambda s: s.copy(c_bvb[:, :], c_bias[:, 2 * DC:3 * DC]),
         reads=["bias"], writes=["bvb"])
    for j in range(2):
        def mmbo(t, j=j):
            for cc in range(DC):
                inst = t.matmul(psA[1][0:1, j * 512:(j + 1) * 512],
                                c_bvb[:, cc:cc + 1],
                                wto[:, cc, j * 512:(j + 1) * 512],
                                start=(cc == 0), stop=(cc == DC - 1))
            return inst
        P.op("tensor", mmbo, reads=["bvb"] + wto_keys, writes=[f"psA1.j{j}"])
    P.op("vector", lambda v: v.tensor_add(c_borow[0:1, :], c_borow[0:1, :], psA[1][0:1, 0:D]),
         reads=["psA1.j0", "psA1.j1", "borow"], writes=["borow"])
    for j in range(2):
        P.op("tensor", lambda t, j=j: t.matmul(psA[0][:, j * 512:(j + 1) * 512],
                                               c_ones[0:1, :],
                                               c_borow[0:1, j * 512:(j + 1) * 512],
                                               start=True, stop=True),
             reads=["borow", "ones"], writes=[f"psA0.j{j}"])
    P.op("vector", lambda v: v.tensor_copy(c_bobc[:, :], psA[0][:, 0:D]),
         reads=["psA0.j0", "psA0.j1"], writes=["bobc"])

    P.op("scalar", lambda s: s.activation(gwsq[:, :], gw_tmp[:, :], AF.Square,
                                          accum_out=c_st[0:E_EXP, 0:1]),
         reads=["gwtmp"], writes=["gwsq", "g.ss"])
    P.op("scalar", lambda s: s.activation(c_st[0:E_EXP, 1:2], c_st[0:E_EXP, 0:1], AF.Ln),
         reads=["g.ss"], writes=["g.ln"])
    P.op("scalar", lambda s: s.activation(c_st[0:E_EXP, 1:2], c_st[0:E_EXP, 1:2], AF.Exp, scale=-0.5),
         reads=["g.ln"], writes=["g.ln"])
    P.op("vector", lambda v: v.tensor_scalar(gwsq[:, :], gw_tmp[:, :], c_st[0:E_EXP, 1:2],
                                             None, ALU.mult),
         reads=["gwtmp", "g.ln"], writes=["gwsq"])
    P.op("scalar", lambda s: s.copy(c_gwbf[:, :], gwsq[:, :]),
         reads=["gwsq"], writes=["gwbf"])
    P.op("tensor", lambda t: [t.transpose(ps7b[:, dc * 16:(dc + 1) * 16],
                                          c_gwbf[0:E_EXP, dc * 128:(dc + 1) * 128],
                                          c_identb[0:E_EXP, 0:E_EXP]) for dc in range(DC)][-1],
         reads=["gwbf", "identb"], writes=["ps7b"])
    P.op("vector", lambda v: v.tensor_copy(c_gnt[:, :], ps7b[:, 0:DC * E_EXP]),
         reads=["ps7b"], writes=["gnt"])

    BQ, BK = 0, DC

    def ln_stats_ops(pump_or_P, xap, sqbuf, sqkey, slot, xkey, sum_on_scalar=False):
        co = 8 + 4 * slot
        s_sum = c_st[:, co:co + 1]
        s_var = c_st[:, co + 1:co + 2]
        s_r = c_st[:, co + 2:co + 3]
        s_b = c_st[:, co + 3:co + 4]
        kS, kV, kR, kB = (f"t{co}", f"t{co + 1}", f"t{co + 2}", f"t{co + 3}")
        if isinstance(pump_or_P, Pump):
            A = pump_or_P.add
        else:
            A = lambda eng, fn, reads=(), writes=(): pump_or_P.op(eng, fn, reads=reads, writes=writes)
        if sum_on_scalar:
            A("scalar", lambda s: s.activation(sqbuf, xap, AF.Identity,
                                               accum_out=s_sum),
              reads=[xkey], writes=[sqkey, kS])
        else:
            A("vector", lambda v: v.reduce_sum(s_sum, xap, axis=AX),
              reads=[xkey], writes=[kS])
        A("scalar", lambda s: s.activation(sqbuf, xap, AF.Square, accum_out=s_var),
          reads=[xkey], writes=[sqkey, kV])
        A("vector", lambda v: v.tensor_scalar(s_b, s_sum, s_sum, 1.0 / (D * D),
                                              ALU.mult, ALU.mult),
          reads=[kS], writes=[kB])
        A("vector", lambda v: v.scalar_tensor_tensor(s_var, s_var, 1.0 / D, s_b,
                                                     ALU.mult, ALU.subtract),
          reads=[kV, kB], writes=[kV])
        A("scalar", lambda s: s.activation(s_r, s_var, AF.Ln, bias=EPS_LN),
          reads=[kV], writes=[kR])
        A("scalar", lambda s: s.activation(s_r, s_r, AF.Exp, scale=-0.5),
          reads=[kR], writes=[kR])
        A("vector", lambda v: v.scalar_tensor_tensor(s_b, s_sum, -1.0 / D, s_r,
                                                     ALU.mult, ALU.mult),
          reads=[kS, kR], writes=[kB])
        return s_r, s_b, s_var, kR, kB, kV

    pump = Pump(P)

    def p1_load(g):
        b = g % NXG
        for n in range(4):
            if g == 0:
                xg_alias = [f"wn_oh{n}"]
            elif g == 1 and not ag:
                xg_alias = [f"wt_o.r{2 * n}", f"wt_o.r{2 * n + 1}",
                            "psA1.j0", "psA1.j1"]
            else:
                xg_alias = []
            dma("gpsimd", xg[b][:, n * D:(n + 1) * D],
                xrot[(g * 4 + n) * 128:(g * 4 + n + 1) * 128, :],
                writes=[f"xg{b}n{n}"] + xg_alias)

    def p1_chain(g):
        b = g % NXG
        bl = g % len(ln1g)
        for n in range(4):
            xap = xg[b][:, n * D:(n + 1) * D]
            sq = p1_sq[n % 2]
            r, bb, _, kR, kB, _ = ln_stats_ops(pump, xap, sq[:, :], f"p1sq{n % 2}",
                                               (g * 4 + n) % 4, f"xg{b}n{n}")
            if ag:
                lg_alias = wto_keys + ["psA1.j0", "psA1.j1"] if (g < 2 and n == 0) else []
            else:
                lg_alias = wvn_keys if (g < 2 and n == 0) else []
            pump.add("scalar", lambda s, n=n, bl=bl, r=r, bb=bb, xap=xap: s.activation(
                ln1g[bl][:, n * D:(n + 1) * D], xap, AF.Identity, bias=bb, scale=r),
                reads=[f"xg{b}n{n}", kR, kB], writes=[f"ln1g{bl}.n{n}"] + lg_alias)

    def p1_transpose(g):
        b = g % len(ln1g)
        lb = g % NLT
        for n in range(4):
            bk, bkey = tbank[n % 2]
            if ag:
                lt_alias = (wvn_keys if (g <= 1 and n == 0) else []) \
                    + (["gwtmp", "gwsq"] if (g == 2 and n == 0) else [])
            else:
                lt_alias = ["gwtmp", "gwsq"] if (g == 0 and n == 0) else []
            P.op("tensor", lambda t, n=n, b=b, bk=bk: [t.transpose(
                bk[:, dc * 128:(dc + 1) * 128],
                ln1g[b][:, n * D + dc * 128:n * D + (dc + 1) * 128],
                c_identb[:, :]) for dc in range(DC)][-1],
                reads=[f"ln1g{b}.n{n}", "identb"], writes=[bkey])
            P.op("vector", lambda v, n=n, lb=lb, bk=bk: v.tensor_copy(
                ln1t[lb][:, :, n * 128:(n + 1) * 128],
                bk.rearrange("p (dc j) -> p dc j", dc=DC)),
                reads=[bkey], writes=[f"ln1t{lb}.n{n}"] + lt_alias)

    def p1_k(g):
        lb = g % NLT
        lt_keys = [f"ln1t{lb}.n{n}" for n in range(4)]
        if ag and NG >= 2:
            HW2 = nt_core // 2
            h = (g * 512) // HW2
            kdst_ap = kag_inh[h][:, g * 512 - h * HW2:(g + 1) * 512 - h * HW2]
        elif ag:
            kdst_ap = kag_in[:, g * 512:(g + 1) * 512]
        else:
            kdst_ap = kscr[:, g * 512:(g + 1) * 512]
        kb = g % 2
        for ic in range(DC):
            pb = ic % 2
            def mmk(t, ic=ic, lb=lb, pb=pb):
                for dc in range(DC):
                    inst = t.matmul(psB[pb][:, :], WTK[:, dc, ic * 128:(ic + 1) * 128],
                                    ln1t[lb][:, dc, :], start=(dc == 0), stop=(dc == DC - 1))
                return inst
            P.op("tensor", mmk, reads=lt_keys + wtk_keys, writes=[f"psB{pb}"])
            P.op("scalar", lambda s, ic=ic, pb=pb, kb=kb: s.activation(
                kst[kb][:, ic, :], psB[pb][:, :], AF.Identity,
                bias=c_bias[:, BK + ic:BK + ic + 1], scale=1.0),
                reads=[f"psB{pb}", "bias"], writes=[f"kst{kb}.{ic}"])
            pump.pump(2)
        dma("sync", kdst_ap.rearrange("(ic p) k -> p ic k", p=128),
            kst[kb][:, :, :], reads=[f"kst{kb}.{ic}" for ic in range(DC)],
            writes=[f"Kscr.g{g}"])

    def p1_v(g):
        lb = g % NLT
        vb = (g % 2) if ag else 0
        vdst = vag_in if ag else vscr
        for n in range(4):
            for j in range(2):
                pb = (n * 2 + j) % 2
                def mmv(t, n=n, j=j, lb=lb, pb=pb):
                    for dc in range(DC):
                        inst = t.matmul(psB[pb][:, :],
                                        ln1t[lb][:, dc, n * 128:(n + 1) * 128],
                                        WVOT[:, dc, j * 512:(j + 1) * 512],
                                        start=(dc == 0), stop=(dc == DC - 1))
                    return inst
                P.op("tensor", mmv, reads=[f"ln1t{lb}.n{n}"] + wvot_keys, writes=[f"psB{pb}"])
                P.op("scalar", lambda s, n=n, j=j, pb=pb, vb=vb: s.copy(
                    vst[vb][:, n, j * 512:(j + 1) * 512], psB[pb][:, :]),
                    reads=[f"psB{pb}"], writes=[f"vst{vb}.{n}.{j}"])
                pump.pump(2)
        dma("sync", vdst[g * 512:(g + 1) * 512, :].rearrange("(n p) d -> p n d", p=128),
            vst[vb][:, :, :], reads=[f"vst{vb}.{n}.{j}" for n in range(4) for j in range(2)],
            writes=[f"Vscr.g{g}"])

    def p1_q(g):
        lb = g % NLT
        lt_keys = [f"ln1t{lb}.n{n}" for n in range(4)]
        for ic in range(DC):
            pb = ic % 2
            def mmq(t, ic=ic, lb=lb, pb=pb):
                for dc in range(DC):
                    inst = t.matmul(psB[pb][:, :], WTQ[:, dc, ic * 128:(ic + 1) * 128],
                                    ln1t[lb][:, dc, :], start=(dc == 0), stop=(dc == DC - 1))
                return inst
            P.op("tensor", mmq, reads=lt_keys + wtq_keys, writes=[f"psB{pb}"])
            valias = [f"vst1.{n}.{j}" for n in range(4) for j in range(2)] \
                if (ag and ic < 2) else []
            P.op("scalar", lambda s, ic=ic, g=g, pb=pb: s.activation(
                qt_sb[:, ic, g * 512:(g + 1) * 512], psB[pb][:, :], AF.Identity,
                bias=c_bias[:, BQ + ic:BQ + ic + 1], scale=1.0),
                reads=[f"psB{pb}", "bias"], writes=[f"qt.g{g}.i{ic}"] + valias)
            pump.pump(2)

    def p1_mm(g):
        p1_k(g)
        p1_v(g)
        if not ag and g < NGQ:
            p1_q(g)

    def kv_load(kc, b, first=False):
        galias, valias = [], []
        if first and ag:
            galias = [f"xg0n{n}" for n in range(4)] if b == 0 else [f"ln1g{bb}.n{n}" for bb in range(2) for n in range(4)]
            valias = ([f"ln1t{bb}.n{n}" for bb in (0, 1) for n in range(4)] if b == 0 else
                      [f"ln1t{bb}.n{n}" for bb in (2, 3) for n in range(4)] + ["gwtmp", "gwsq"])
        elif first:
            galias = [f"xg{b}n{n}" for n in range(4)]
            valias = ([f"ln1g{bb}.n{n}" for bb in range(2) for n in range(4)] if b == 0 else
                      [f"ln1t{bb}.n{n}" for bb in range(NLT) for n in range(4)] + ["gwtmp", "gwsq"])
        if ag and NG >= 2:
            HW2 = nt_core // 2
            nparts = max(1, KCHUNK // HW2)
            plen = KCHUNK // nparts
            for pi in range(nparts):
                off_g = kc * KCHUNK + pi * plen
                rank = off_g // nt_core
                off = off_g % nt_core
                h = off // HW2
                hoff = off % HW2
                ksrc = kag_out[h][rank * D:(rank + 1) * D, hoff:hoff + plen]
                vsrc = vag_out[h][rank * HW2 + hoff:rank * HW2 + hoff + plen, :]
                dma("sync", kbuf[b][:, :, pi * plen:(pi + 1) * plen],
                    ksrc.rearrange("(ic p) k -> p ic k", p=128),
                    reads=[f"kago{h}"],
                    writes=[f"kbuf{b}"] + (galias if pi == 0 else []))
            for pi in range(nparts):
                off_g = kc * KCHUNK + pi * plen
                rank = off_g // nt_core
                off = off_g % nt_core
                h = off // HW2
                hoff = off % HW2
                vsrc = vag_out[h][rank * HW2 + hoff:rank * HW2 + hoff + plen, :]
                dma("sync", vbuf[b][:, pi * (plen // 128):(pi + 1) * (plen // 128), :],
                    vsrc.rearrange("(n p) d -> p n d", p=128),
                    reads=[f"vago{h}"],
                    writes=[f"vbuf{b}"] + (valias if pi == 0 else []))
            return
        elif ag:
            rank = (kc * KCHUNK) // nt_core
            off = (kc * KCHUNK) % nt_core
            ksrc = kag_outF[rank * D:(rank + 1) * D, off:off + KCHUNK]
            vsrc = vag_outF[kc * KCHUNK:(kc + 1) * KCHUNK, :]
            krd, vrd = ["kago0"], ["vago0"]
        else:
            ksrc = kscr[:, kc * KCHUNK:(kc + 1) * KCHUNK]
            vsrc = vscr[kc * KCHUNK:(kc + 1) * KCHUNK, :]
            krd = [f"Kscr.g{gg}" for gg in range(kc * KCHUNK // 512, (kc + 1) * KCHUNK // 512)]
            vrd = [f"Vscr.g{gg}" for gg in range(kc * KCHUNK // 512, (kc + 1) * KCHUNK // 512)]
        dma("sync", kbuf[b][:, :, :], ksrc.rearrange("(ic p) k -> p ic k", p=128),
            reads=krd, writes=[f"kbuf{b}"] + galias)
        dma("sync", vbuf[b][:, :, :], vsrc.rearrange("(n p) d -> p n d", p=128),
            reads=vrd, writes=[f"vbuf{b}"] + valias)

    p1_load(0)
    p1_chain(0)
    pump.drain()
    kv_early = [False]
    if ag:
        HW = nt_core // 2

        def ag_k(h):
            P.op("gpsimd", lambda g, h=h: g.collective_compute(
                "AllGather", ALU.bypass, replica_groups=ag_groups,
                ins=[kag_inh[h].ap().opt()],
                outs=[kag_out[h].ap().opt()]),
                reads=[f"Kscr.g{g}" for g in range(h * NG // 2, (h + 1) * NG // 2)]
                if SPLIT_AG else [f"Kscr.g{g}" for g in range(NG)],
                writes=[f"kago{h}"])

        def ag_v(h):
            P.op("gpsimd", lambda g, h=h: g.collective_compute(
                "AllGather", ALU.bypass, replica_groups=ag_groups,
                ins=[vag_in[h * HW:(h + 1) * HW, :].opt()],
                outs=[vag_out[h].ap().opt()]),
                reads=[f"Vscr.g{g}" for g in range(h * NG // 2, (h + 1) * NG // 2)]
                if SPLIT_AG else [f"Vscr.g{g}" for g in range(NG)],
                writes=[f"vago{h}"])

        if NG < 2:
            def ag_k(h, _ag_k=ag_k):
                if h == 0:
                    P.op("gpsimd", lambda g: g.collective_compute(
                        "AllGather", ALU.bypass, replica_groups=ag_groups,
                        ins=[kag_in.ap().opt()], outs=[kag_outF.ap().opt()]),
                        reads=[f"Kscr.g{g}" for g in range(NG)], writes=["kago0", "kago1"])

            def ag_v(h, _ag_v=ag_v):
                if h == 0:
                    P.op("gpsimd", lambda g: g.collective_compute(
                        "AllGather", ALU.bypass, replica_groups=ag_groups,
                        ins=[vag_in.ap().opt()], outs=[vag_outF.ap().opt()]),
                        reads=[f"Vscr.g{g}" for g in range(NG)], writes=["vago0", "vago1"])
        if SPLIT_AG:
            for g in range(NG // 2):
                if g + 1 < NG:
                    p1_load(g + 1)
                    p1_chain(g + 1)
                p1_transpose(g)
                p1_k(g)
                pump.drain()
            ag_k(0)
            for g in range(NG // 2):
                if NG // 2 + g + 1 < NG:
                    p1_load(NG // 2 + g + 1)
                    p1_chain(NG // 2 + g + 1)
                p1_v(g)
                pump.drain()
            ag_v(0)
            for g in range(NG // 2, NG):
                p1_transpose(g)
                p1_k(g)
                pump.drain()
            ag_k(1)
            for g in range(NG // 2, NG):
                p1_v(g)
                pump.drain()
            ag_v(1)
        else:
            for g in range(NG):
                if g + 1 < NG:
                    p1_load(g + 1)
                    p1_chain(g + 1)
                p1_transpose(g)
                p1_k(g)
                pump.drain()
            ag_k(0)
            ag_k(1)
            for g in range(NG):
                p1_v(g)
            ag_v(0)
            ag_v(1)
        for g in range(NG):
            p1_q(g)
    else:
        for g in range(NG):
            if g + 1 < NG:
                p1_load(g + 1)
                p1_chain(g + 1)
            p1_transpose(g)
            if g == NG - 1 and NG >= 4 and NKC > 1:
                kv_load(0, 0, first=True)
                kv_early[0] = True
            p1_mm(g)
            pump.drain()

    if taps:
        dma("sync", taps_t["t_att"][:, (NG - 1) * 512:NG * 512].rearrange("(ic p) k -> p ic k", p=128),
            kst[(NG - 1) % 2][:, :, :],
            reads=[f"kst{(NG - 1) % 2}.{ic}" for ic in range(DC)], writes=["tapatt"])

    CST_GRP = 64

    def st4(gp, off):
        return c_st[:, 64 + gp * 48 + off:64 + gp * 48 + off + 4]

    def st1(gp, off, q):
        return c_st[:, 64 + gp * 48 + off + q:64 + gp * 48 + off + q + 1]

    def bc16(gp, off):
        return bass.AP(c_st, 64 + gp * 48 + off, [[160, 128], [1, 4], [0, 16]])

    def oacc3d(gi):
        return oaccA[:, 4 * gi * D:(4 * gi + 4) * D].rearrange(
            "p (t d) -> p t d", t=4)

    def p3_groupA_head(gi):
        gp = gi % 2
        ts4 = list(range(4 * gi, 4 * gi + 4))
        ok = [f"oacc{t}" for t in ts4]
        for t in ts4:
            pump.add("sync", lambda g, t=t: g.dma_start(
                p3_xr[t % 4][:, :], xrot[t * 128:(t + 1) * 128, :]),
                writes=[f"p3xr{t % 4}"], dma=True)
            pump.add("vector", lambda v, t=t: v.scalar_tensor_tensor(
                out_acc[t], out_acc[t], c_rl[:, t:t + 1], c_bobc[:, :],
                ALU.mult, ALU.add),
                reads=[f"rlt.{t // QT_PER_G}", "bobc"], writes=[f"oacc{t}"])
        kS, kV, kR, kB = (f"gA{gp}.s1", f"gA{gp}.v1", f"gA{gp}.r1", f"gA{gp}.b1")
        pump.add("vector", lambda v, gi=gi, gp=gp: v.reduce_sum(
            st4(gp, 0), oacc3d(gi), axis=AX), reads=ok, writes=[kS])
        for t in ts4:
            pump.add("scalar", lambda s, t=t, gp=gp: s.activation(
                p3_sq[t % 2][:, :], out_acc[t], AF.Square,
                accum_out=st1(gp, 4, t % 4)),
                reads=[f"oacc{t}"], writes=[f"p3sq{t % 2}", kV] if t % 4 == 3 else [f"p3sq{t % 2}", kV])
        pump.add("vector", lambda v, gp=gp: v.scalar_tensor_tensor(
            st4(gp, 12), st4(gp, 0), 1.0 / (D * D), st4(gp, 0), ALU.mult, ALU.mult),
            reads=[kS], writes=[kB])
        pump.add("vector", lambda v, gp=gp: v.scalar_tensor_tensor(
            st4(gp, 4), st4(gp, 4), 1.0 / D, st4(gp, 12), ALU.mult, ALU.subtract),
            reads=[kV, kB], writes=[kV])
        pump.add("scalar", lambda s, gp=gp: s.activation(
            st4(gp, 8), st4(gp, 4), AF.Ln, bias=EPS_LN), reads=[kV], writes=[kR])
        pump.add("scalar", lambda s, gp=gp: s.activation(
            st4(gp, 8), st4(gp, 8), AF.Exp, scale=-0.5), reads=[kR], writes=[kR])
        pump.add("vector", lambda v, gp=gp: v.scalar_tensor_tensor(
            st4(gp, 12), st4(gp, 0), -1.0 / D, st4(gp, 8), ALU.mult, ALU.mult),
            reads=[kS, kR], writes=[kB])
        for t in ts4:
            pump.add("scalar", lambda s, t=t, gp=gp: s.activation(
                p3_ln[t % 2][:, :], out_acc[t], AF.Identity,
                bias=st1(gp, 12, t % 4), scale=st1(gp, 8, t % 4)),
                reads=[f"oacc{t}", kR, kB], writes=[f"p3ln{t % 2}"])
            pump.add("vector", lambda v, t=t: v.tensor_add(
                out_acc[t], p3_xr[t % 4][:, :], p3_ln[t % 2][:, :]),
                reads=[f"p3xr{t % 4}", f"p3ln{t % 2}"], writes=[f"oacc{t}"])
            pump.add("sync", lambda g, t=t: g.dma_start(
                out[t * 128:(t + 1) * 128, :], out_acc[t]),
                reads=[f"oacc{t}"], writes=[f"outb.{t}"], dma=True)
            if taps:
                pump.add("sync", lambda g, t=t: g.dma_start(
                    x1scr[t * 128:(t + 1) * 128, :], out_acc[t]),
                    reads=[f"oacc{t}"], writes=[f"x1s.{t}"], dma=True)
        kS2, kV2, kR2, kB2 = (f"gA{gp}.s2", f"gA{gp}.v2", f"gA{gp}.r2", f"gA{gp}.b2")
        kLA, kLD = f"gA{gp}.lA", f"gA{gp}.ld"
        pump.add("vector", lambda v, gi=gi, gp=gp: v.reduce_sum(
            st4(gp, 16), oacc3d(gi), axis=AX), reads=ok, writes=[kS2])
        for t in ts4:
            pump.add("scalar", lambda s, t=t, gp=gp: s.activation(
                p3_sq[t % 2][:, :], out_acc[t], AF.Square,
                accum_out=st1(gp, 20, t % 4)),
                reads=[f"oacc{t}"], writes=[f"p3sq{t % 2}", kV2])
        pump.add("vector", lambda v, gp=gp: v.scalar_tensor_tensor(
            st4(gp, 28), st4(gp, 16), 1.0 / (D * D), st4(gp, 16), ALU.mult, ALU.mult),
            reads=[kS2], writes=[kB2])
        pump.add("vector", lambda v, gp=gp: v.scalar_tensor_tensor(
            st4(gp, 20), st4(gp, 20), 1.0 / D, st4(gp, 28), ALU.mult, ALU.subtract),
            reads=[kV2, kB2], writes=[kV2])
        pump.add("scalar", lambda s, gp=gp: s.activation(
            st4(gp, 24), st4(gp, 20), AF.Ln, bias=EPS_LN), reads=[kV2], writes=[kR2])
        pump.add("scalar", lambda s, gp=gp: s.activation(
            st4(gp, 32), st4(gp, 20), AF.Ln, scale=float(D), bias=EPS_L2),
            reads=[kV2], writes=[kLA])
        pump.add("vector", lambda v, gp=gp: v.tensor_sub(
            st4(gp, 36), st4(gp, 32), st4(gp, 24)), reads=[kLA, kR2], writes=[kLD])
        pump.add("scalar", lambda s, gp=gp: s.activation(
            st4(gp, 36), st4(gp, 36), AF.Exp, scale=-0.5), reads=[kLD], writes=[kLD])
        pump.add("scalar", lambda s, gp=gp: s.activation(
            st4(gp, 24), st4(gp, 24), AF.Exp, scale=-0.5), reads=[kR2], writes=[kR2])
        pump.add("vector", lambda v, gp=gp: v.scalar_tensor_tensor(
            st4(gp, 28), st4(gp, 16), -1.0 / D, st4(gp, 24), ALU.mult, ALU.mult),
            reads=[kS2, kR2], writes=[kB2])
    def p3_groupA_tail(gi):
        gp = gi % 2
        ts4 = list(range(4 * gi, 4 * gi + 4))
        kR2, kB2 = f"gA{gp}.r2", f"gA{gp}.b2"
        for t in ts4:
            al = [f"kst1.{2 * (t % 4)}", f"kst1.{2 * (t % 4) + 1}"] if gi == 0 else []
            pump.add("scalar", lambda s, t=t, gp=gp: s.activation(
                p3_ln2[t % 4][:, :], out_acc[t], AF.Identity,
                bias=st1(gp, 28, t % 4), scale=st1(gp, 24, t % 4)),
                reads=[f"oacc{t}", kR2, kB2], writes=[f"p3ln2{t % 4}"] + al)

    def p3_groupA(gi):
        p3_groupA_head(gi)
        p3_groupA_tail(gi)

    if not kv_early[0]:
        kv_load(KC_ORDER[0], 0, first=True)
    if NKC > 1:
        kv_load(KC_ORDER[1], 1, first=True)

    P.fence("scalar", [f"kst{kb}.{ic}" for kb in range(2) for ic in range(DC)]
            + [f"vst{vb}.{n}.{j}" for vb in range(2) for n in range(4) for j in range(2)]
            + (["tapatt"] if taps else []))

    for ki in range(NKC):
        kc = KC_ORDER[ki]
        b = ki % 2
        last = (ki == NKC - 1) and NKC > 1
        if last:
            for _e in ("vector", "scalar", "sync"):
                P.fence(_e, ["kbuf0", "vbuf0"])
        def emit_scores(qg, ks, kc=kc, b=b, last=last):
            ptb = (kc * NQG + qg) % 2
            pb = ks % 2
            qgq_keys = [f"qt.g{qg}.i{ic}" for ic in range(DC)]
            def mms(t, ks=ks, qg=qg, b=b, pb=pb):
                for dc in range(DC):
                    inst = t.matmul(psB[pb][:, 0:QG],
                                    kbuf[b][:, dc, ks * 128:(ks + 1) * 128],
                                    qt_sb[:, dc, qg * QG:(qg + 1) * QG],
                                    start=(dc == 0), stop=(dc == DC - 1))
                return inst
            P.op("tensor", mms, reads=[f"kbuf{b}"] + qgq_keys, writes=[f"psB{pb}"])
            mcol = kc * KS + ks
            P.op("scalar", lambda s, ks=ks, pb=pb, ptb=ptb, mcol=mcol: s.activation(
                pt[ptb][ks][:, :], psB[pb][:, 0:QG], AF.Exp,
                bias=-CSHIFT, scale=c_mask[:, mcol:mcol + 1]),
                reads=[f"psB{pb}", "mask"], writes=[f"pt{ptb}.{ks}"])
            if last:
                pump.pump(2)

        def emit_out(qg, qt, kc=kc, b=b, ki=ki, last=last):
            ptb = (kc * NQG + qg) % 2
            pa = qt % 2
            qglob = qg * QT_PER_G + qt
            def mmo(t, qt=qt, b=b, pa=pa, ptb=ptb):
                for ks in range(KS):
                    st = (ks == 0)
                    sp = (ks == KS - 1)
                    t.matmul(psA[pa][:, 0:512], pt[ptb][ks][:, qt * 128:(qt + 1) * 128],
                             vbuf[b][:, ks, 0:512], start=st, stop=sp)
                    t.matmul(psA[pa][:, 512:1024], pt[ptb][ks][:, qt * 128:(qt + 1) * 128],
                             vbuf[b][:, ks, 512:1024], start=st, stop=sp)
                    inst = t.matmul(psB[2][:, qt:qt + 1],
                                    pt[ptb][ks][:, qt * 128:(qt + 1) * 128],
                                    c_onescol[:, 0:1], start=st, stop=sp)
                return inst
            oalias = (wtq_keys + wtk_keys + wvot_keys) if ki == 0 else []
            P.op("tensor", mmo,
                 reads=[f"pt{ptb}.{ks}" for ks in range(KS)] + [f"vbuf{b}", "onescol"],
                 writes=[f"psA{pa}", "psB2"])
            if ki == 0:
                P.op("vector", lambda v, pa=pa, qglob=qglob: v.tensor_copy(
                    out_acc[qglob], psA[pa][:, 0:D]),
                    reads=[f"psA{pa}"], writes=[f"oacc{qglob}"] + oalias)
            else:
                P.op("vector", lambda v, pa=pa, qglob=qglob: v.tensor_add(
                    out_acc[qglob], out_acc[qglob], psA[pa][:, 0:D]),
                    reads=[f"psA{pa}", f"oacc{qglob}"], writes=[f"oacc{qglob}"])
            if last:
                pump.pump(3)

        for ks in range(KS):
            emit_scores(0, ks)
        for qg in range(NQG):
            kpq = KS // QT_PER_G
            for qt in range(QT_PER_G):
                if qg + 1 < NQG:
                    for kk in range(kpq * qt, kpq * (qt + 1)):
                        emit_scores(qg + 1, kk)
                emit_out(qg, qt)
            if ki == 0:
                P.op("vector", lambda v, qg=qg: v.tensor_copy(
                    c_lacc[:, qg * QT_PER_G:(qg + 1) * QT_PER_G], psB[2][:, 0:QT_PER_G]),
                    reads=["psB2"], writes=[f"lacc.{qg}"])
            else:
                P.op("vector", lambda v, qg=qg: v.tensor_add(
                    c_lacc[:, qg * QT_PER_G:(qg + 1) * QT_PER_G],
                    c_lacc[:, qg * QT_PER_G:(qg + 1) * QT_PER_G], psB[2][:, 0:QT_PER_G]),
                    reads=["psB2", f"lacc.{qg}"],
                    writes=[f"lacc.{qg}"])
            if last:
                sl = slice(qg * QT_PER_G, (qg + 1) * QT_PER_G)
                P.op("scalar", lambda s, sl=sl: s.activation(c_rl[:, sl], c_lacc[:, sl], AF.Ln),
                     reads=[f"lacc.{qg}"], writes=[f"rla.{qg}"])
                P.op("scalar", lambda s, sl=sl: s.activation(c_rl[:, sl], c_rl[:, sl],
                                                             AF.Exp, scale=-1.0),
                     reads=[f"rla.{qg}"], writes=[f"rlt.{qg}"])
                if qg == 0:
                    p3_groupA(0)
        if ki + 2 < NKC:
            kv_load(KC_ORDER[ki + 2], b)

    if NKC == 1:
        for _e in ("vector", "scalar", "sync"):
            P.fence(_e, ["kbuf0", "vbuf0"])
        for qg in range(NQG):
            sl = slice(qg * QT_PER_G, (qg + 1) * QT_PER_G)
            P.op("scalar", lambda s, sl=sl: s.activation(c_rl[:, sl], c_lacc[:, sl], AF.Ln),
                 reads=[f"lacc.{qg}"], writes=[f"rla.{qg}"])
            P.op("scalar", lambda s, sl=sl: s.activation(c_rl[:, sl], c_rl[:, sl],
                                                         AF.Exp, scale=-1.0),
                 reads=[f"rla.{qg}"], writes=[f"rlt.{qg}"])
        p3_groupA(0)
        pump.drain()

    pump.drain()

    dma("gpsimd", wgbsh[:, :, :], swg[:, :].rearrange("(dc p) h -> p dc h", p=128),
        writes=["wgbsh", "kbuf1"] + [f"xg1n{n}" for n in range(4)] + wto_keys)
    dma("gpsimd", wubsh[:, :, :], swu[:, :].rearrange("(dc p) h -> p dc h", p=128),
        writes=["wubsh", "kbuf1"] + [f"xg1n{n}" for n in range(4)] + wto_keys)
    dma("gpsimd", wdbsh[:, :, :], swd[:, :].rearrange("(hc p) d -> p hc d", p=128),
        writes=["wdbsh", "vbuf1", "gwtmp", "gwsq"]
        + [f"ln1t{bb}.n{n}" for bb in range(2) for n in range(4)])

    def p3_groupB(gi):
        gp = gi % 2
        ts4 = list(range(4 * gi, 4 * gi + 4))
        G4 = c_g4[gp]
        for t in ts4:
            i = t % 2
            q = t % 4
            bk, bkey = tbank[i]
            P.op("tensor", lambda t_, q=q, bk=bk: [t_.transpose(
                bk[:, dc * 128:(dc + 1) * 128],
                p3_ln2[q][:, dc * 128:(dc + 1) * 128], c_identb[:, :]) for dc in range(DC)][-1],
                reads=[f"p3ln2{q}", "identb"], writes=[bkey])
            P.op("vector", lambda v, t=t, bk=bk: v.tensor_copy(
                ln2t[:, :, t * 128:(t + 1) * 128],
                bk.rearrange("p (dc j) -> p dc j", dc=DC)),
                reads=[bkey], writes=[f"ln2t.{t}"] + [f"qt.g{t // 4}.i{ic}" for ic in range(DC)])
            def mmg(t_, t=t, i=i):
                for dc in range(DC):
                    inst = t_.matmul(psB[i][:, 0:E_EXP], ln2t[:, dc, t * 128:(t + 1) * 128],
                                     c_gnt[:, dc * E_EXP:(dc + 1) * E_EXP],
                                     start=(dc == 0), stop=(dc == DC - 1))
                return inst
            P.op("tensor", mmg, reads=[f"ln2t.{t}", "gnt"], writes=[f"psB{i}"])
            P.op("vector", lambda v, G4=G4, i=i, q=q, gp=gp: v.tensor_scalar(
                G4[:, q * 64:q * 64 + E_EXP], psB[i][:, 0:E_EXP],
                st1(gp, 36, q), None, ALU.mult),
                reads=[f"psB{i}", f"gA{gp}.ld"], writes=[f"g4log{gp}"])
        def v3(lo):
            return G4[:, :].rearrange("p (t f) -> p t f", t=4)[:, :, lo:lo + E_EXP]
        kSg, kWk, kM = f"g4sig{gp}", f"g4wk{gp}", f"g4m{gp}"
        thr = st4(gp, 40)
        thrb = bc16(gp, 40)
        pump.add("scalar", lambda s, G4=G4: s.activation(
            v3(16), v3(0), AF.Sigmoid), reads=[f"g4log{gp}"], writes=[kSg])
        pump.add("vector", lambda v, G4=G4: v.tensor_copy(v3(32), v3(16)),
                 reads=[kSg], writes=[kWk])
        for it in range(TOPK):
            pump.add("vector", lambda v, G4=G4, thr=thr: v.reduce_max(
                thr, v3(32), axis=AX), reads=[kWk], writes=[kM])
            if it < TOPK - 1:
                pump.add("vector", lambda v, G4=G4, thrb=thrb: v.tensor_tensor(
                    v3(48), v3(32), thrb, ALU.is_ge), reads=[kWk, kM], writes=[f"g4t{gp}"])
                pump.add("vector", lambda v, G4=G4: v.scalar_tensor_tensor(
                    v3(32), v3(48), -BIG, v3(32), ALU.mult, ALU.add),
                    reads=[f"g4t{gp}", kWk], writes=[kWk])
        pump.add("vector", lambda v, G4=G4, thrb=thrb: v.tensor_tensor(
            v3(32), v3(16), thrb, ALU.is_ge), reads=[kSg, kM], writes=[kWk])
        pump.add("vector", lambda v, G4=G4: v.tensor_tensor(
            v3(32), v3(32), v3(16), ALU.mult), reads=[kWk, kSg], writes=[kWk])
        pump.add("vector", lambda v, G4=G4, thr=thr: v.reduce_sum(
            thr, v3(32), axis=AX), reads=[kWk], writes=[kM])
        pump.add("scalar", lambda s, thr=thr: s.activation(thr, thr, AF.Ln),
                 reads=[kM], writes=[kM])
        pump.add("scalar", lambda s, thr=thr: s.activation(thr, thr, AF.Exp, scale=-1.0),
                 reads=[kM], writes=[kM])
        ckeys = [f"comb.{t}" for t in ts4]
        pump.add("vector", lambda v, G4=G4, gi=gi, thrb=thrb: v.tensor_tensor(
            c_comb[:, 4 * gi * E_EXP:(4 * gi + 4) * E_EXP].rearrange(
                "p (t f) -> p t f", t=4),
            v3(32), thrb, ALU.mult),
            reads=[kWk, kM], writes=ckeys)

    def ln2t_gkeys(j):
        return [f"ln2t.{tt}" for tt in range(4 * j, 4 * j + 4)]

    act_first = set()
    ew_first = [True]
    lt1k = [f"ln1t1.n{n}" for n in range(4)]
    slot_alias_map = {0: None, 1: None, 2: None, 3: None,
                      4: ["vbuf1"] + lt1k, 5: ["vbuf1"] + lt1k}

    def gu_pair(wgG, wuG, wgkey, wukey, hc, j, it2, aslot, has_cb, cbx, aalias=()):
        aalias = list(aalias)
        if aslot not in act_first:
            act_first.add(aslot)
            sa = slot_alias_map[aslot]
            if sa is None:
                sa = [f"pt{bb}.{k}" for bb in range(2) for k in range(KS)]
            aalias += sa
        pg = (it2 * 2) % 3
        pu = (it2 * 2 + 1) % 3
        P.op("tensor", lambda t_, hc=hc, j=j, pg=pg, wgG=wgG: [t_.matmul(
            psB[pg][:, :], wgG[:, dc, hc * 128:(hc + 1) * 128],
            ln2t[:, dc, j * 512:(j + 1) * 512],
            start=(dc == 0), stop=(dc == DC - 1)) for dc in range(DC)][-1],
            reads=[wgkey] + ln2t_gkeys(j), writes=[f"psB{pg}"])
        ewal = ["p1sq1"] if ew_first[0] else []
        ew_first[0] = False
        P.op("scalar", lambda s, pg=pg: s.activation(
            ew[0][:, :], psB[pg][:, :], AF.Silu),
            reads=[f"psB{pg}"], writes=["ew0"] + ewal)
        P.op("tensor", lambda t_, hc=hc, j=j, pu=pu, wuG=wuG: [t_.matmul(
            psB[pu][:, :], wuG[:, dc, hc * 128:(hc + 1) * 128],
            ln2t[:, dc, j * 512:(j + 1) * 512],
            start=(dc == 0), stop=(dc == DC - 1)) for dc in range(DC)][-1],
            reads=[wukey] + ln2t_gkeys(j), writes=[f"psB{pu}"])
        P.op("vector", lambda v, pu=pu, aslot=aslot, j=j: v.tensor_tensor(
            actt[aslot][:, j * 512:(j + 1) * 512], ew[0][:, :], psB[pu][:, :], ALU.mult),
            reads=["ew0", f"psB{pu}"], writes=[f"act.{aslot}.{j}"] + list(aalias) + ewal)

    def p5_tile(t):
        i = t % 2
        al_x = ["wgb0"] if t == 0 else []
        r, bb, _, kR, kB, _ = ln_stats_ops(P, routed[t][:, :], p1_sq[0][:, :],
                                           "p1sq0", 4 + 2 * i, f"routed{t}")
        P.op("scalar", lambda s, t=t, i=i, r=r, bb=bb: s.activation(
            p3_af[i][:, :], routed[t][:, :], AF.Identity, bias=bb, scale=r),
            reads=[f"routed{t}", kR, kB], writes=[f"p3af{i}"] + al_x)
        P.op("gpsimd", lambda g, t=t, i=i: g.dma_start(
            out[t * 128:(t + 1) * 128, :], p3_af[i][:, :], accum_op=ALU.add),
            reads=[f"p3af{i}", f"outb.{t}"], writes=[f"outw.{t}"], dma=True)

    def down_unit(u_first, aslots, wdG, wdkey, e_comb=None, tail=None, tiles=None):
        for t in (range(NT) if tiles is None else tiles):
            act_keys = [f"act.{a}.{t // 4}" for a in aslots]
            pa = t % 2
            def mmd(t_, t=t, pa=pa, aslots=aslots, wdG=wdG):
                for hi, a in enumerate(aslots):
                    st = (hi == 0)
                    sp = (hi == HC - 1)
                    t_.matmul(psA[pa][:, 0:512], actt[a][:, t * 128:(t + 1) * 128],
                              wdG[:, hi, 0:512], start=st, stop=sp)
                    inst = t_.matmul(psA[pa][:, 512:1024], actt[a][:, t * 128:(t + 1) * 128],
                                     wdG[:, hi, 512:1024], start=st, stop=sp)
                return inst
            P.op("tensor", mmd, reads=act_keys + [wdkey], writes=[f"psA{pa}"])
            if e_comb is None:
                if u_first:
                    P.op("vector", lambda v, pa=pa, t=t: v.tensor_copy(
                        routed[t][:, :], psA[pa][:, 0:D]),
                        reads=[f"psA{pa}"], writes=[f"routed{t}", f"oacc{t}"])
                else:
                    P.op("vector", lambda v, pa=pa, t=t: v.tensor_add(
                        routed[t][:, :], routed[t][:, :], psA[pa][:, 0:D]),
                        reads=[f"psA{pa}", f"routed{t}"], writes=[f"routed{t}"])
            else:
                P.op("vector", lambda v, pa=pa, t=t, e=e_comb: v.scalar_tensor_tensor(
                    routed[t][:, :], psA[pa][:, 0:D],
                    c_comb[:, t * E_EXP + e:t * E_EXP + e + 1],
                    routed[t][:, :], ALU.mult, ALU.add),
                    reads=[f"psA{pa}", f"routed{t}", f"comb.{t}"], writes=[f"routed{t}"])
            if tail is not None:
                tail(t)

    SH_SLOTS = [0, 1, 2, 3]
    for j in range(NGT):
        if j + 1 < NGT:
            p3_groupA_head(j + 1)
        p3_groupB(j)
        if j + 1 < NGT:
            p3_groupA_tail(j + 1)
        for hc in range(HC):
            gu_pair(wgbsh, wubsh, "wgbsh", "wubsh", hc, j, j * HC + hc,
                    SH_SLOTS[hc], False, 0)
            pump.pump(12)
        down_unit(True, SH_SLOTS, wdbsh, "wdbsh",
                  tiles=range(4 * j, min(4 * j + 4, NT)))
        pump.drain()

    if taps:
        P.op("sync", lambda g: g.dma_start(taps_t["t_x1"][:, :], x1scr[:, :]),
             reads=[f"x1s.{t}" for t in range(NT)], writes=["tapx1"], dma=True)
        P.op("sync", lambda g: g.dma_start(taps_t["t_cb"][:, :], c_comb[:, :]),
             reads=[f"comb.{t}" for t in range(NT)], writes=["tapcb"], dma=True)

    def wload(e, b):
        al_g = (["p3xr2", "p3xr3", "p3af0", "p3af1", "kbuf0"]) if e < 2 else []
        al_u = ([f"p3sq{i}" for i in range(2)] + ["wgbsh", "wubsh", "kbuf1"]) if e < 2 else []
        dma("gpsimd", wgb[b][:, :, :],
            wgate[e:e + 1, :, :].rearrange("e (dc p) h -> p (e dc) h", p=128),
            writes=[f"wgb{b}"] + al_g)
        dma("gpsimd", wub[b][:, :, :],
            wup[e:e + 1, :, :].rearrange("e (dc p) h -> p (e dc) h", p=128),
            writes=[f"wub{b}"] + (["wubsh"] if e < 2 else []) + al_u)

    def wdload(e):
        wb = e % 2
        if e == 0:
            al = [f"p3ln{i}" for i in range(2)]
        elif e == 1:
            al = (["wdbsh", "vbuf1", "gwtmp", "gwsq"]
                  + [f"ln1t{bb}.n{n}" for bb in range(NLT) for n in range(4)])
        else:
            al = []
        dma("gpsimd", wdb[wb][:, :, :],
            wdown[e:e + 1, :, :].rearrange("e (hc p) d -> p (e hc) d", p=128),
            writes=[f"wdb{wb}"] + al)

    wload(0, 0)
    wdload(0)
    for e in range(E_EXP):
        b = e % 2
        if e + 1 < E_EXP:
            wload(e + 1, (e + 1) % 2)
            wdload(e + 1)
        aslots = [(4 + e * HC + hc) % 6 for hc in range(HC)]
        if e < E_EXP - 1:
            for hc in range(HC):
                for j in range(NGT):
                    gu_pair(wgb[b], wub[b], f"wgb{b}", f"wub{b}", hc, j,
                            hc * NGT + j, aslots[hc], True, 0)
            if e < E_EXP - 2:
                down_unit(False, aslots, wdb[b], f"wdb{b}", e_comb=e)
        else:
            asl_prev = [(4 + (e - 1) * HC + hc) % 6 for hc in range(HC)]
            for j in range(NGT):
                down_unit(False, asl_prev, wdb[1 - b], f"wdb{1 - b}",
                          e_comb=e - 1, tiles=range(4 * j, 4 * j + 4))
                for hc in range(HC):
                    gu_pair(wgb[b], wub[b], f"wgb{b}", f"wub{b}", hc, j,
                            j * HC + hc, aslots[hc], True, 0)
                down_unit(False, aslots, wdb[b], f"wdb{b}", e_comb=e,
                          tail=p5_tile, tiles=range(4 * j, 4 * j + 4))

    if taps:
        for t in range(NT):
            dma("sync", taps_t["t_ffn"][t * 128:(t + 1) * 128, :], routed[t][:, :],
                reads=[f"routed{t}"], writes=[f"tapffn{t}"])

    for q in Prog.DMAQ:
        for i in range(Prog.NSLOT):
            ch = f"{q}.d{i}"
            if P.cnt[ch]:
                P.ops[q].append(lambda g, s=P.sem[ch], c=P.cnt[ch]: g.wait_ge(s, c))

    P.render()
    return nc, taps_t


def _mask_pp(s_batch, q0):
    pos = (q0 + np.arange(s_batch)) % S
    m = np.exp(-0.5 * ((pos - S / 2.0) / (S / 4.0)) ** 2) / np.sqrt(D)
    return np.ascontiguousarray(m.reshape(s_batch // 128, 128).T.astype(np.float32))


def make_inputs(x, Wq, bq, Wk, bk, Wv, bv, Wo, bo, gate_w, w_gate, w_up, w_down,
                sw_gate, sw_up, sw_down, nt_core, s_batch, cores, ag=True):
    bqa = np.asarray(bq, np.float32).reshape(DC, 128).T
    bka = np.asarray(bk, np.float32).reshape(DC, 128).T
    bva = np.asarray(bv, np.float32).reshape(DC, 128).T
    boa = np.asarray(bo, np.float32).reshape(DC, 128).T
    bqkvo = np.ascontiguousarray(np.concatenate([bqa, bka, bva, boa], axis=1).astype(np.float32))
    constsin = np.broadcast_to(np.array(CONSTS, np.float32), (128, len(CONSTS))).copy()
    common = dict(
        wq=np.ascontiguousarray(Wq, np.float32), wk=np.ascontiguousarray(Wk, np.float32),
        wv=np.ascontiguousarray(Wv, np.float32), wo=np.ascontiguousarray(Wo, np.float32),
        bqkvo=bqkvo,
        borow=np.ascontiguousarray(np.asarray(bo, np.float32).reshape(1, D)),
        gatew=np.ascontiguousarray(gate_w, np.float32),
        wgate=np.ascontiguousarray(w_gate, np.float32),
        wup=np.ascontiguousarray(w_up, np.float32),
        wdown=np.ascontiguousarray(w_down, np.float32),
        swg=np.ascontiguousarray(sw_gate, np.float32),
        swu=np.ascontiguousarray(sw_up, np.float32),
        swd=np.ascontiguousarray(sw_down, np.float32),
        identf=np.eye(128, dtype=np.float32),
        identb=np.eye(128, dtype=np.float32).astype(ml_dtypes.bfloat16),
        onesrow=np.ones((1, 128), np.float32),
        onescol=np.ones((128, 1), np.float32).astype(ml_dtypes.bfloat16),
        constsin=constsin,
    )
    in_maps = []
    for core in range(cores):
        bidx = core // (cores // B)
        q0 = (core % (cores // B)) * nt_core
        im = dict(common)
        if ag:
            im["xrot"] = np.ascontiguousarray(x[bidx][q0:q0 + nt_core])
            im["maskpp"] = _mask_pp(s_batch, 0)
        else:
            im["xrot"] = np.ascontiguousarray(np.roll(x[bidx][:s_batch], -q0, axis=0))
            im["maskpp"] = _mask_pp(s_batch, q0)
        in_maps.append(im)
    return in_maps


def kernel(L0_attn_x, L0_attn_Wq, L0_attn_bq, L0_attn_Wk, L0_attn_bk,
           L0_attn_Wv, L0_attn_bv, L0_attn_Wo, L0_attn_bo,
           L0_ffn_gate_w, L0_ffn_w_gate, L0_ffn_w_up, L0_ffn_w_down,
           L0_ffn_sw_gate, L0_ffn_sw_up, L0_ffn_sw_down):
    x = np.ascontiguousarray(np.asarray(L0_attn_x, dtype=np.float32))
    nt_core = B * S // NCORES
    nc, _ = build(nt_core=nt_core, s_batch=S, taps=False, ag=True)
    in_maps = make_inputs(x, L0_attn_Wq, L0_attn_bq, L0_attn_Wk, L0_attn_bk,
                          L0_attn_Wv, L0_attn_bv, L0_attn_Wo, L0_attn_bo,
                          L0_ffn_gate_w, L0_ffn_w_gate, L0_ffn_w_up, L0_ffn_w_down,
                          L0_ffn_sw_gate, L0_ffn_sw_up, L0_ffn_sw_down,
                          nt_core, S, NCORES)
    res = run_bass_kernel_spmd(nc, in_maps, core_ids=list(range(NCORES)))
    outa = np.empty((B, S, D), np.float32)
    for core in range(NCORES):
        bidx = core // (NCORES // B)
        q0 = (core % (NCORES // B)) * nt_core
        outa[bidx, q0:q0 + nt_core, :] = np.asarray(res.results[core]["out"])
    return outa
